# version 12
# baseline (speedup 1.0000x reference)
import numpy as np
import ml_dtypes

import concourse.bass as bass
import concourse.mybir as mybir
from concourse.bass_utils import run_bass_kernel_spmd

F32 = mybir.dt.float32
BF16 = mybir.dt.bfloat16
I32 = mybir.dt.int32
AF = mybir.ActivationFunctionType
ALU = mybir.AluOpType
AXX = mybir.AxisListType.X
PoolE = mybir.EngineType.Pool
POOL = [PoolE]

B, L, D = 4, 4096, 2048
HEADS, HD = 16, 128
RANK = 16
LORA_SCALE = 0.5
ATT_SCALE = float(HD) ** -0.5
EXP_SHIFT = -16.0
DT_RANK, NST = 128, 16
NEXP, FF = 8, 1024
TOK = 2048
NBLK = TOK // 128
DTL = D // 128
LN_EPS = 1e-5

N_STAGES = 4
STOP_AT = 99
WO_PARTS = 7
MOE_NE = NEXP



def _bf(a):
    return np.ascontiguousarray(np.asarray(a)).astype(ml_dtypes.bfloat16)


def _f32(a):
    return np.ascontiguousarray(np.asarray(a), dtype=np.float32)


def _tiled_T(w):
    w = np.asarray(w)
    return np.ascontiguousarray(w.reshape(w.shape[0] // 128, 128, w.shape[1]).transpose(0, 2, 1))


def make_inputs(core, full, nonce):
    b, par = core // 2, core & 1
    x = np.asarray(full["L0_attn_a_x"])
    blocks = [x[b, gg * 128:(gg + 1) * 128, :] for gg in range(par, 2 * NBLK, 2)]
    xc = np.concatenate(blocks, axis=0)
    d = {
        "x_in": _f32(xc.T),
        "role": np.array([[par, 0]], np.int32),
        "flagv": (nonce + 1 + np.arange(64, dtype=np.int64)).astype(np.int32).reshape(1, 64),
    }
    for nm in ("q", "k", "v", "o"):
        d[f"W{nm}T"] = _bf(_tiled_T(full[f"L0_attn_a_W{nm}"]))
        d[f"B{nm}T"] = _bf(np.asarray(full[f"L0_attn_a_B{nm}"]).T)
        d[f"A{nm}T"] = _bf(np.asarray(full[f"L0_attn_a_A{nm}"]).T)
        d[f"b{nm}r"] = _bf(np.asarray(full[f"L0_attn_a_b{nm}"]).reshape(1, D))
    d["rw"] = _bf(full["L0_ffn_router_w"])
    d["gate"] = _bf(full["L0_ffn_gate_proj"])
    d["up"] = _bf(full["L0_ffn_up_proj"])
    d["down"] = _bf(np.stack([_tiled_T(np.asarray(full["L0_ffn_down_proj"])[e].T)
                              for e in range(NEXP)]))
    xpwT = np.asarray(full["L0_attn_b_x_proj_weight"]).T
    d["dt_down"] = _bf(xpwT[:, 0:DT_RANK])
    d["WBC"] = _bf(xpwT[:, DT_RANK:DT_RANK + 2 * NST])
    d["dtwT"] = _bf(_tiled_T(full["L0_attn_b_dt_proj_weight"]))
    d["dt_bias"] = _f32(np.asarray(full["L0_attn_b_dt_proj_bias"]).reshape(DTL, 128, 1))
    d["A_log"] = _f32(full["L0_attn_b_A_log"])
    d["Dvec"] = _f32(np.asarray(full["L0_attn_b_D"]).reshape(DTL, 128, 1))
    d["opwT"] = _bf(_tiled_T(full["L0_attn_b_out_proj_weight"]))
    d["obias"] = _f32(np.asarray(full["L0_attn_b_out_proj_bias"]).reshape(DTL, 128, 1))
    return d


def reassemble(outs):
    yy = np.zeros((B, L, D), np.float32)
    for c in range(8):
        b, par = c // 2, c & 1
        yc = np.asarray(outs[c]["y_out"]).T
        for i, gg in enumerate(range(par, 2 * NBLK, 2)):
            yy[b, gg * 128:(gg + 1) * 128, :] = yc[i * 128:(i + 1) * 128, :]
    return yy



class Sem:
    __slots__ = ("h", "n")

    def __init__(self, nc, name):
        self.h = nc.alloc_semaphore(name)
        self.n = 0

    def inc(self, inst, k=1):
        inst.then_inc(self.h, k)
        self.n += k
        return inst


class Prog:
    def __init__(self, nc):
        self.nc = nc
        self.t, self.v, self.s, self.g, self.y = (nc.tensor, nc.vector, nc.scalar,
                                                  nc.gpsimd, nc.sync)
        self.T = Sem(nc, "semT")
        self.V = Sem(nc, "semV")
        self.S = Sem(nc, "semS")
        self.G = Sem(nc, "semG")
        self.Y = Sem(nc, "semY")
        self._lbl = 0
        self.build()

    def lbl(self, base):
        self._lbl += 1
        return f"{base}{self._lbl}"

    def wT(self, eng):
        eng.wait_ge(self.T.h, self.T.n)

    def wV(self, eng):
        eng.wait_ge(self.V.h, self.V.n)

    def wS(self, eng):
        eng.wait_ge(self.S.h, self.S.n)

    def wG(self, eng):
        eng.wait_ge(self.G.h, self.G.n)

    def build(self):
        nc = self.nc
        t, v, s, g = self.t, self.v, self.s, self.g
        T, V, S, G = self.T, self.V, self.S, self.G

        def din(name, shape, dt_):
            return nc.dram_tensor(name, shape, dt_, kind="ExternalInput").ap()

        self.xin = din("x_in", [D, TOK], F32)
        self.role_in = din("role", [1, 2], I32)
        self.flagv_in = din("flagv", [1, 64], I32)
        self.yout = nc.dram_tensor("y_out", [D, TOK], F32, kind="ExternalOutput").ap()

        self.W = {}
        for nm, shape, dt_ in [
            ("WqT", [DTL, D, 128], BF16), ("WkT", [DTL, D, 128], BF16),
            ("WvT", [DTL, D, 128], BF16), ("WoT", [DTL, D, 128], BF16),
            ("BqT", [D, RANK], BF16), ("BkT", [D, RANK], BF16),
            ("BvT", [D, RANK], BF16), ("BoT", [D, RANK], BF16),
            ("AqT", [RANK, D], BF16), ("AkT", [RANK, D], BF16),
            ("AvT", [RANK, D], BF16), ("AoT", [RANK, D], BF16),
            ("bqr", [1, D], BF16), ("bkr", [1, D], BF16),
            ("bvr", [1, D], BF16), ("bor", [1, D], BF16),
            ("rw", [D, NEXP], BF16),
            ("gate", [NEXP, D, FF], BF16), ("up", [NEXP, D, FF], BF16),
            ("down", [NEXP, DTL, FF, 128], BF16),
            ("dt_down", [D, DT_RANK], BF16), ("WBC", [D, 2 * NST], BF16),
            ("dtwT", [DTL, DT_RANK, 128], BF16), ("dt_bias", [DTL, 128, 1], F32),
            ("A_log", [D, NST], F32), ("Dvec", [DTL, 128, 1], F32),
            ("opwT", [DTL, D, 128], BF16), ("obias", [DTL, 128, 1], F32),
        ]:
            self.W[nm] = din(nm, shape, dt_)

        def shr(name, rows):
            return nc.dram_tensor(name, [rows, TOK], BF16, kind="Internal",
                                  addr_space="Shared").ap()

        self.sh_kv = shr("sh_kv", 2 * 2 * HEADS * 128)
        self.sh_dx = shr("sh_dx", 2 * 2 * DTL * 128)
        self.sh_bc = shr("sh_bc", 2 * 2 * NST)
        self.sh_y = shr("sh_y", 2 * 8 * 128)
        self.sh_fl = nc.dram_tensor("sh_fl", [2, 64], I32, kind="Internal",
                                    addr_space="Shared").ap()

        A = nc.alloc_sbuf_tensor
        self.xs = A("xs", [128, DTL * TOK], BF16)
        self.ar1 = A("ar1", [128, DTL * TOK], BF16)
        self.ar2 = A("ar2", [128, 20480], BF16)
        self.ar3 = A("ar3", [128, 11264], BF16)
        self.rows = A("rows", [128, 1024], F32)
        self.rowsb = A("rowsb", [128, 2048], BF16)
        self.bca = A("bca", [128, 2560], BF16)
        self.rbv = self.ar2[:, 0:2048]
        self.nmb = self.ar2[:, 2048:4096]
        self.x2t = self.ar2[:, 4096:6144]
        self.misc = A("misc", [128, 512], BF16)
        self.role_t = A("role_t", [1, 2], I32)
        self.flag_t = A("flag_t", [1, 64], I32)
        self.fpoll = A("fpoll", [1, 2], I32)
        self.acol = A("acol", [128, 320], F32)

        self.P2A = nc.alloc_psum_tensor("P2A", [128, 1024], F32)
        self.P2B = nc.alloc_psum_tensor("P2B", [128, 1024], F32)
        self.P2C = nc.alloc_psum_tensor("P2C", [128, 1024], F32)
        self.P1A = nc.alloc_psum_tensor("P1A", [128, 512], F32)
        self.PBT = nc.alloc_psum_tensor("PBT", [128, 1024], BF16)

        self.idn = self.misc[:, 0:128]
        self.ones_c = self.misc[:, 128:130]
        self.zo_c = self.misc[:, 130:132]
        self.on33 = self.misc[:, 160:193]
        self.zz33 = self.misc[:, 193:226]
        self.iotaf = self.misc[:, 132:148].bitcast(F32)
        self.onesr = None

        riota = self.rows[:, 0:128]
        g.iota(riota, [[1, 128]], channel_multiplier=-1, allow_small_or_imprecise_dtypes=True)
        g.iota(self.iotaf, [[1, 8]], channel_multiplier=0, allow_small_or_imprecise_dtypes=True)
        g.memset(self.ones_c[:, 0:1], 1.0)
        g.memset(self.ones_c[:, 1:2], 0.0)
        g.memset(self.zo_c[:, 0:1], 0.0)
        g.memset(self.zo_c[:, 1:2], 1.0)
        g.memset(self.on33[:, 1:33], 0.0)
        g.memset(self.on33[:, 0:1], 1.0)
        g.memset(self.zz33[:, 0:32], 0.0)
        g.memset(self.zz33[:, 32:33], 1.0)
        self.onesb = self.bca[0:1, 2048:2560]
        for bb in (0, 32, 64, 96):
            g.memset(self.bca[bb:bb + 1, 2048:2560], 1.0)
            g.memset(self.acol[bb:bb + 1, 128:256], 1.0)
        g.memset(self.acol[:, 256:257], LN_EPS)
        g.memset(self.acol[:, 257:258], EXP_SHIFT)
        self.CSET = Sem(nc, "cset")
        self.CSET.inc(g.memset(self.acol[:, 258:259], 1.0))
        nc.const_aps.register(F32, LN_EPS, self.acol[:, 256:257])
        nc.const_aps.register(F32, EXP_SHIFT, self.acol[:, 257:258])
        nc.const_aps.register(F32, 1.0, self.acol[:, 258:259])
        G.inc(g.dma_start(self.role_t[:], self.role_in[:]), 16)
        G.inc(g.dma_start(self.flag_t[:], self.flagv_in[:]), 16)
        self.wG(v)
        v.wait_ge(self.CSET.h, self.CSET.n)
        V.inc(v.tensor_single_scalar(self.idn, riota, 0.0, ALU.is_equal))
        self.wG(g)
        self.wG(t)
        self.wG(s)
        t.wait_ge(self.CSET.h, self.CSET.n)
        s.wait_ge(self.CSET.h, self.CSET.n)
        g.wait_ge(self.CSET.h, self.CSET.n)

        self.role_r = nc.alloc_registers("role_r", engines=POOL)
        nc.reg_load(self.role_r[PoolE], self.role_t[:, 0:1])
        self.exp_r = nc.alloc_registers("exp_r", engines=POOL)
        self.f_r = nc.alloc_registers("f_r", engines=POOL)
        self.pw_r = nc.alloc_registers("pw_r", engines=POOL)
        nc.regs_mov(self.pw_r, 0)
        self.psem = Sem(nc, "psem")
        self.ev_sent = 0
        self.ev_wait = 0

        for d in range(DTL):
            G.inc(g.dma_start(self.xs_t(d), self.xin[d * 128:(d + 1) * 128, :]), 16)
        self.wG(t)
        self.wG(v)
        self.wG(s)

        if N_STAGES >= 1:
            self.layernorm()
            if STOP_AT >= 2:
                self.attn_a()
        if N_STAGES >= 2:
            self.layernorm()
            self.moe()
        if N_STAGES >= 3:
            self.layernorm()
            self.s6()
        if N_STAGES >= 4:
            self.layernorm()
            self.moe()

        self.wV(g)
        self.wT(g)
        self.wS(g)
        stw = Sem(nc, "stw")
        for d in range(DTL):
            stw.inc(g.dma_start(self.yout[d * 128:(d + 1) * 128, :], self.xs_t(d)), 16)
        g.wait_ge(stw.h, stw.n)

    def xs_t(self, d):
        return self.xs[:, d * TOK:(d + 1) * TOK]

    def lna_t(self, d):
        return self.ar1[:, d * TOK:(d + 1) * TOK]

    def pair_write(self, dst_fn, src_ap, sem):
        nc, g = self.nc, self.g
        le, lo, ldn = self.lbl("pwE"), self.lbl("pwO"), self.lbl("pwD")
        nc.br_eq(self.role_r, 0, le, lo, engines=POOL)
        with nc.body(le, valid_engines=POOL):
            sem.inc(g.dma_start(dst_fn(0), src_ap), 16)
            g.br(ldn)
        with nc.body(lo, valid_engines=POOL):
            g.dma_start(dst_fn(1), src_ap).then_inc(sem.h, 16)
            g.br(ldn)
        nc.switch_bb(ldn)

    def send_flag(self):
        g = self.g
        ev = self.ev_sent
        self.ev_sent += 1
        self.pair_write(lambda r: self.sh_fl[r:r + 1, ev:ev + 1],
                        self.flag_t[:, ev:ev + 1], self.G)
        g.wait_ge(self.G.h, self.G.n)

    def wait_flag(self):
        nc, g = self.nc, self.g
        ev = self.ev_wait
        self.ev_wait += 1
        nc.reg_load(self.exp_r[PoolE], self.flag_t[:, ev:ev + 1])
        le, lo, ldn = self.lbl("plE"), self.lbl("plO"), self.lbl("plD")
        nc.br_eq(self.role_r, 0, le, lo, engines=POOL)
        for lbl, slot in ((le, 1), (lo, 0)):
            with nc.body(lbl, valid_engines=POOL):
                nc.regs_alu(self.pw_r, self.pw_r, 16, op=ALU.add)
                g.dma_start(self.fpoll[:, 0:1],
                            self.sh_fl[slot:slot + 1, ev:ev + 1]).then_inc(self.psem.h, 16)
                g.wait_ge(self.psem.h, self.pw_r)
                nc.reg_load(self.f_r[PoolE], self.fpoll[:, 0:1])
                nc.br_cmp(self.f_r, self.exp_r, lbl, lbl + "x", "IS_NE", engines=POOL)
            with nc.body(lbl + "x", valid_engines=POOL):
                g.br(ldn)
        nc.switch_bb(ldn)

    def layernorm(self):
        t, v, s = self.t, self.v, self.s
        T, V, S = self.T, self.V, self.S
        P2A, P2B = self.P2A, self.P2B
        onesb16 = self.bca[0:1, 2048:2176]
        nrow = self.rows[0:1, 0:512]
        rrow = self.rows[0:1, 512:1024]
        nrow_b = self.ar3[0:1, 0:512]
        rrow_b = self.ar3[0:1, 512:1024]
        self.wV(t)
        self.wS(t)
        for ci in range(4):
            csl = slice(ci * 512, (ci + 1) * 512)
            for d in range(DTL):
                t.matmul(P2A[0:1, 0:512], self.ones_c[:, 0:1], self.xs_t(d)[:, csl],
                         start=(d == 0), stop=(d == DTL - 1))
            for d in range(DTL):
                self.wT(v)
                V.inc(v.tensor_mul(self.x2t[:, 0:512], self.xs_t(d)[:, csl],
                                   self.xs_t(d)[:, csl]))
                self.wV(t)
                T.inc(t.matmul(P2A[0:1, 512:1024], self.ones_c[:, 0:1], self.x2t[:, 0:512],
                               start=(d == 0), stop=(d == DTL - 1)))
            self.wT(v)
            self.wS(v)
            V.inc(v.tensor_copy(nrow, P2A[0:1, 0:512]))
            V.inc(v.tensor_copy(rrow, P2A[0:1, 512:1024]))
            self.wV(s)
            S.inc(s.activation(nrow, nrow, AF.Identity, scale=-1.0 / D))
            self.wS(s)
            S.inc(s.activation(nrow_b, nrow, AF.Identity))
            self.wS(v)
            m2 = self.ar3[0:1, 1024:2048].bitcast(F32)
            V.inc(v.tensor_mul(m2, nrow, nrow))
            self.wV(v)
            V.inc(v.scalar_tensor_tensor(rrow, rrow, 1.0 / D, m2, ALU.mult, ALU.subtract))
            self.wV(s)
            S.inc(s.activation(rrow, rrow, AF.Ln, bias=LN_EPS))
            self.wS(s)
            S.inc(s.activation(rrow_b, rrow, AF.Exp, scale=-0.5))
            self.wS(t)
            self.wV(t)
            T.inc(t.matmul(P2B[:, 0:512], onesb16, nrow_b, start=True, stop=True))
            T.inc(t.matmul(P2B[:, 512:1024], onesb16, rrow_b, start=True, stop=True))
            self.wT(v)
            V.inc(v.tensor_copy(self.nmb[:, csl], P2B[:, 0:512]))
            V.inc(v.tensor_copy(self.rbv[:, csl], P2B[:, 512:1024]))
            self.wV(t)
        self.wV(v)
        for d in range(DTL):
            V.inc(v.tensor_add(self.lna_t(d), self.xs_t(d), self.nmb[:, :]))
            self.wV(v)
            V.inc(v.tensor_mul(self.lna_t(d), self.lna_t(d), self.rbv[:, :]))
            self.wV(v)

    def attn_a(self):
        nc = self.nc
        t, v, s, g = self.t, self.v, self.s, self.g
        T, V, S, G = self.T, self.V, self.S, self.G
        P2A, P2B, P2C, P1A, PBT = self.P2A, self.P2B, self.P2C, self.P1A, self.PBT
        W = self.W
        wk = self.ar2
        ar3 = self.ar3
        ATTO, KF, VTM, QH, WS = 0, 4096, 8192, 12288, 14336
        ld = Sem(nc, "aald")
        self.wT(g)
        self.wV(g)
        slots3 = {"q": 0, "k": 32, "v": 64}
        bias_rows = {}
        for nm, bb in (("q", 0), ("k", 32), ("v", 64)):
            dst = self.bca[bb:bb + 1, 0:2048]
            ld.inc(g.dma_start(dst, W[f"b{nm}r"][:]), 16)
            bias_rows[nm] = dst
        bsl = {}
        for i, nm in enumerate(("q", "k", "v", "o")):
            dst = ar3[:, i * 256:(i + 1) * 256]
            ld.inc(g.dma_start(dst.rearrange("p (kt r) -> p kt r", kt=DTL),
                               W[f"B{nm}T"][:].rearrange("(kt p) r -> p kt r", p=128)), 16)
            bsl[nm] = dst
        asl = {}
        for nm, bb in (("q", 0), ("k", 32), ("v", 64)):
            dst = ar3[bb:bb + 16, 1024:3072]
            ld.inc(g.dma_start(dst, W[f"A{nm}T"][:]), 16)
            asl[nm] = dst
        t.wait_ge(ld.h, ld.n)
        s.wait_ge(ld.h, ld.n)
        self.wV(t)
        if STOP_AT < 3:
            return

        xbr = {}
        slots = {"q": 0, "k": 32, "v": 64}
        for nm in ("q", "k", "v"):
            for tc in range(4):
                for kt in range(DTL):
                    mm = t.matmul(P1A[0:16, 0:512], bsl[nm][:, kt * 16:(kt + 1) * 16],
                                  self.lna_t(kt)[:, tc * 512:(tc + 1) * 512],
                                  start=(kt == 0), stop=(kt == DTL - 1))
                T.inc(mm)
                self.wT(s)
                dst = self.rowsb[slots[nm]:slots[nm] + 16, tc * 512:(tc + 1) * 512]
                S.inc(s.activation(dst, P1A[0:16, 0:512], AF.Identity, scale=LORA_SCALE))
                self.wS(t)
            xbr[nm] = self.rowsb[slots[nm]:slots[nm] + 16, 0:2048]

        def proj(psum, wsl_ap, xb_rows, bias_slice, aslc, tc, base):
            for kt in range(DTL):
                t.matmul(psum[:, 0:512], wsl_ap[:, kt * 128:(kt + 1) * 128],
                         self.lna_t(kt)[:, tc * 512:(tc + 1) * 512],
                         start=(kt == 0), stop=False)
            t.matmul(psum[:, 0:512], bias_slice,
                     self.bca[base:base + 1, 2048:2560], start=False, stop=False)
            return t.matmul(psum[:, 0:512], aslc, xb_rows[:, tc * 512:(tc + 1) * 512],
                            start=False, stop=True)

        self.apb = {}
        kvw = Sem(nc, "kvw")
        lkv = Sem(nc, self.lbl("ldkv"))
        kv_ready = {}
        kv_t_after = {0: T.n, 1: T.n}
        WSB = {0: WS, 1: KF}
        y = self.y
        KVE = y

        def load_kv(h):
            slot = h % 2
            if kv_t_after[slot]:
                KVE.wait_ge(T.h, kv_t_after[slot])
            base = WSB[slot]
            lkv.inc(KVE.dma_start(wk[:, base:base + 2048].rearrange("p (kt c) -> p kt c", kt=DTL),
                                W["WkT"][h].rearrange("(kt p) c -> p kt c", p=128)), 16)
            lkv.inc(KVE.dma_start(wk[:, base + 2048:base + 4096].rearrange("p (kt c) -> p kt c", kt=DTL),
                                W["WvT"][h].rearrange("(kt p) c -> p kt c", p=128)), 16)
            kv_ready[h] = lkv.n

        load_kv(0)
        load_kv(1)
        tpm = [0] * 8
        vpm = [0, 0, 0, 0]
        for h in range(HEADS):
            base = WSB[h % 2]
            t.wait_ge(lkv.h, kv_ready[h])
            v.wait_ge(kvw.h, kvw.n)
            for pj, wsoff, outoff in (("k", 0, QH), ("v", 2048, ATTO)):
                for tc in range(4):
                    psum = P2A if tc % 2 == 0 else P2B
                    key = "A0" if tc % 2 == 0 else "B0"
                    self.pwait(t, key)
                    mm = proj(psum, wk[:, base + wsoff:base + wsoff + 2048], xbr[pj],
                              bias_rows[pj][:, h * 128:h * 128 + 128],
                              asl[pj][:, h * 128:(h + 1) * 128], tc, slots3[pj])
                    T.inc(mm)
                    self.wT(v)
                    V.inc(v.tensor_copy(wk[:, outoff + tc * 512:outoff + (tc + 1) * 512],
                                        psum[:, 0:512]))
                    self.apb[key] = (V.h, V.n)
                    if pj == "v":
                        vpm[tc] = V.n
            kv_t_after[h % 2] = T.n
            if h + 2 < HEADS:
                load_kv(h + 2)
            self.wV(g)
            self.pair_write(
                lambda r, h=h: self.sh_kv[(r * 2 + 0) * HEADS * 128 + h * 128:
                                          (r * 2 + 0) * HEADS * 128 + (h + 1) * 128, :],
                wk[:, QH:QH + 2048], kvw)
            for jg in range(2):
                self.wV(t)
                for j8 in range(8):
                    j = jg * 8 + j8
                    T.inc(t.transpose(PBT[:, j8 * 128:(j8 + 1) * 128],
                                      wk[:, ATTO + j * 128:ATTO + (j + 1) * 128], self.idn))
                self.wT(v)
                V.inc(v.tensor_copy(wk[:, VTM + jg * 1024:VTM + (jg + 1) * 1024],
                                    PBT[:, 0:1024]))
            self.wV(g)
            self.pair_write(
                lambda r, h=h: self.sh_kv[(r * 2 + 1) * HEADS * 128 + h * 128:
                                          (r * 2 + 1) * HEADS * 128 + (h + 1) * 128, :],
                wk[:, VTM:VTM + 2048], kvw)
        g.wait_ge(kvw.h, kvw.n)
        if STOP_AT < 4:
            return
        self.send_flag()
        self.wait_flag()
        if STOP_AT < 5:
            return

        sc = Sem(nc, "sc")
        ex = Sem(nc, "ex")
        av = Sem(nc, "av")
        lqw = Sem(nc, "lqw")
        lkf = Sem(nc, "lkf")
        lvt = Sem(nc, "lvt")
        self.wT(y)
        self.wT(g)
        zb = ar3[:, 3072:5120].bitcast(F32)
        xbo_f = ar3[:, 7168:11264].bitcast(F32)
        self.wS(v)
        V.inc(v.memset(xbo_f[0:16, 0:2048], 0.0))
        KQ = 32
        WSQ = {0: WS, 1: WS + 2048}
        qp_mark = {0: 0, 1: 0}
        wq_ready = {}
        kf_ready = {}
        vt_ready = {}

        def load_wq(h):
            if qp_mark[h % 2]:
                y.wait_ge(T.h, qp_mark[h % 2])
            lqw.inc(y.dma_start(
                wk[:, WSQ[h % 2]:WSQ[h % 2] + 2048].rearrange("p (kt c) -> p kt c", kt=DTL),
                W["WqT"][h].rearrange("(kt p) c -> p kt c", p=128)), 16)
            wq_ready[h] = lqw.n

        def load_kf(h, w):
            if w:
                g.wait_ge(w[0], w[1])
            for r in range(2):
                b_ = (r * 2 + 0) * HEADS * 128 + h * 128
                lkf.inc(g.dma_start(wk[:, KF + r * 2048:KF + (r + 1) * 2048],
                                    self.sh_kv[b_:b_ + 128, :]), 16)
            kf_ready[h] = lkf.n

        def load_vt(h, w):
            if w:
                g.wait_ge(w[0], w[1])
            for r in range(2):
                b_ = (r * 2 + 1) * HEADS * 128 + h * 128
                lvt.inc(g.dma_start(wk[:, VTM + r * 2048:VTM + (r + 1) * 2048],
                                    self.sh_kv[b_:b_ + 128, :]), 16)
            vt_ready[h] = lvt.n

        load_wq(0)
        load_wq(1)
        load_kf(0, None)
        load_vt(0, None)
        for h in range(HEADS):
            hq = h % 2
            t.wait_ge(lqw.h, wq_ready[h])
            for tc in range(4):
                psum = P2A if tc % 2 == 0 else P2B
                key = "A0" if tc % 2 == 0 else "B0"
                self.pwait(t, key)
                mm = proj(psum, wk[:, WSQ[hq]:WSQ[hq] + 2048], xbr["q"],
                          bias_rows["q"][:, h * 128:h * 128 + 128],
                          asl["q"][:, h * 128:(h + 1) * 128], tc, 0)
                T.inc(mm)
                v.wait_ge(T.h, T.n)
                V.inc(v.tensor_copy(wk[:, QH + tc * 512:QH + (tc + 1) * 512], psum[:, 0:512]))
                self.apb[key] = (V.h, V.n)
            qp_mark[hq] = T.n
            if h + 2 < HEADS:
                load_wq(h + 2)
            t.wait_ge(lkf.h, kf_ready[h])
            for qc in range(2):
                q0 = qc * 1024
                sc0, ex0, av0 = sc.n, ex.n, av.n
                self.wV(s)
                for k in range(KQ):
                    s.wait_ge(sc.h, sc0 + k + 1)
                    if k >= 2:
                        s.wait_ge(av.h, av0 + k - 1)
                    ptile = ar3[:, 5120 + (k % 2) * 1024:5120 + (k % 2 + 1) * 1024]
                    ex.inc(s.activation(ptile, (P2A if k % 2 == 0 else P2B)[:, 0:1024],
                                        AF.Exp, bias=EXP_SHIFT, scale=ATT_SCALE))
                def avz(k, first):
                    if first:
                        self.pwait(t, "C0", "C1", "P1")
                        t.wait_ge(lvt.h, vt_ready[h])
                    ptile = ar3[:, 5120 + (k % 2) * 1024:5120 + (k % 2 + 1) * 1024]
                    vblk = wk[:, VTM + k * 128:VTM + (k + 1) * 128]
                    t.matmul(P2C[:, 0:512], vblk, ptile[:, 0:512],
                             start=first, stop=(k == KQ - 1))
                    t.matmul(P2C[:, 512:1024], vblk, ptile[:, 512:1024],
                             start=first, stop=(k == KQ - 1))
                    t.matmul(P1A[0:33, 0:512], self.on33, ptile[:, 0:512],
                             start=first, stop=False)
                    mm = t.matmul(P1A[0:33, 0:512], self.zz33, ptile[:, 512:1024],
                                  start=False, stop=(k == KQ - 1))
                    av.inc(mm)
                for k in range(KQ):
                    if k >= 2:
                        t.wait_ge(ex.h, ex0 + k - 1)
                    if k == 0:
                        self.pwait(t, "A0", "A1")
                    if k == 1:
                        self.pwait(t, "B0")
                    ksl = wk[:, KF + k * 128:KF + (k + 1) * 128]
                    t.matmul((P2A if k % 2 == 0 else P2B)[:, 0:512], ksl,
                             wk[:, QH + q0:QH + q0 + 512], start=True, stop=True)
                    mm = t.matmul((P2A if k % 2 == 0 else P2B)[:, 512:1024], ksl,
                                  wk[:, QH + q0 + 512:QH + q0 + 1024], start=True, stop=True)
                    sc.inc(mm)
                    if k >= 1:
                        t.wait_ge(ex.h, ex0 + k)
                        avz(k - 1, first=(k == 1))
                if qc == 1 and h + 1 < HEADS:
                    load_kf(h + 1, (sc.h, sc.n))
                t.wait_ge(ex.h, ex0 + KQ)
                avz(KQ - 1, first=False)
                v.wait_ge(av.h, av.n)
                zr0 = self.rows[0:1, 0:256].bitcast(BF16)
                zr1 = self.rows[32:33, 0:256].bitcast(BF16)
                with nc.allow_low_precision(reason="bf16 1/Z row (within bf16 budget)"):
                    V.inc(v.reciprocal(zr0, P1A[0:1, 0:512]))
                    V.inc(v.reciprocal(zr1, P1A[32:33, 0:512]))
                self.apb["P1"] = (V.h, V.n)
                self.wV(t)
                T.inc(t.matmul(P2A[:, 0:512], self.bca[0:1, 2048:2176], zr0,
                               start=True, stop=True))
                T.inc(t.matmul(P2A[:, 512:1024], self.bca[32:33, 2048:2176], zr1,
                               start=True, stop=True))
                v.wait_ge(T.h, T.n)
                V.inc(v.tensor_copy(zb[:, 0:1024], P2A[:, 0:1024]))
                self.apb["A0"] = self.apb["A1"] = (V.h, V.n)
                self.wV(v)
                V.inc(v.tensor_mul(wk[:, ATTO + hq * 2048 + q0:ATTO + hq * 2048 + q0 + 1024],
                                   P2C[:, 0:1024], zb[:, 0:1024]))
                self.apb["C0"] = self.apb["C1"] = (V.h, V.n)
            if h + 1 < HEADS:
                load_vt(h + 1, (av.h, av.n))
            if hq == 1 and STOP_AT >= 6:
                self._wo_quarter(h // 2, bsl, asl)
        if STOP_AT >= 6 and (WO_PARTS & 4):
            self._wo_final(bias_rows, asl)

    def pwait(self, eng, *keys):
        for k in keys:
            m = self.apb.get(k)
            if m:
                eng.wait_ge(m[0], m[1])

    def _wo_quarter(self, qi, bsl, asl):
        nc = self.nc
        t, v, g, s, y = self.t, self.v, self.g, self.s, self.y
        T, V = self.T, self.V
        wk, ar3 = self.ar2, self.ar3
        ATTO, WOS = 0, 18432
        P2A, P2B, P2C = self.P2A, self.P2B, self.P2C
        xbo_f = ar3[:, 7168:11264].bitcast(F32)
        t.wait_ge(V.h, V.n)
        for tc in range(4 if (WO_PARTS & 1) else 0):
            ps = P2A if tc % 2 == 0 else P2B
            key = "A0" if tc % 2 == 0 else "B0"
            self.pwait(t, key)
            for j in range(2):
                kt = qi * 2 + j
                mm = t.matmul(ps[0:16, 0:512], bsl["o"][:, kt * 16:(kt + 1) * 16],
                              wk[:, ATTO + j * 2048 + tc * 512:ATTO + j * 2048 + (tc + 1) * 512],
                              start=(j == 0), stop=(j == 1))
            T.inc(mm)
            v.wait_ge(T.h, T.n)
            V.inc(v.tensor_add(xbo_f[0:16, tc * 512:(tc + 1) * 512],
                               xbo_f[0:16, tc * 512:(tc + 1) * 512], ps[0:16, 0:512]))
            self.apb[key] = (V.h, V.n)
        if not hasattr(self, "wo_ldw"):
            self.wo_ldw = Sem(nc, "ldwo")
            self.wo_ready = {}
            self.wo_t_after = {0: 0, 1: 0}
        ldw = self.wo_ldw

        def load_wo(ot_):
            slot = ot_ % 2
            if self.wo_t_after[slot]:
                y.wait_ge(T.h, self.wo_t_after[slot])
            ldw.inc(y.dma_start(
                wk[:, WOS + slot * 256:WOS + slot * 256 + 256].rearrange("p (j c) -> p j c", j=2),
                self.W["WoT"][ot_][qi * 256:(qi + 1) * 256, :]
                .rearrange("(j p) c -> p j c", p=128)), 16)
            self.wo_ready[ot_] = ldw.n

        load_wo(0)
        for ot in range(DTL if (WO_PARTS & 2) else 0):
            slot = ot % 2
            t.wait_ge(ldw.h, self.wo_ready[ot])
            if ot + 1 < DTL:
                load_wo(ot + 1)
            for tc in range(4):
                key = "C0" if tc % 2 == 0 else "C1"
                coff = (tc % 2) * 512
                self.pwait(t, key)
                for j in range(2):
                    mm = t.matmul(P2C[:, coff:coff + 512],
                                  wk[:, WOS + slot * 256 + j * 128:WOS + slot * 256 + (j + 1) * 128],
                                  wk[:, ATTO + j * 2048 + tc * 512:ATTO + j * 2048 + (tc + 1) * 512],
                                  start=(j == 0), stop=(j == 1))
                T.inc(mm)
                v.wait_ge(T.h, T.n)
                V.inc(v.tensor_add(self.xs_t(ot)[:, tc * 512:(tc + 1) * 512],
                                   P2C[:, coff:coff + 512],
                                   self.xs_t(ot)[:, tc * 512:(tc + 1) * 512]))
                self.apb[key] = (V.h, V.n)
            self.wo_t_after[slot] = T.n

    def _wo_final(self, bias_rows, asl):
        nc = self.nc
        t, v, s, g = self.t, self.v, self.s, self.g
        T, V, S = self.T, self.V, self.S
        ar3 = self.ar3
        P2C = self.P2C
        xbo_f = ar3[:, 7168:11264].bitcast(F32)
        xbo_b = self.rowsb[0:16, 0:2048]
        aslo = ar3[0:16, 1024:3072]
        bo_row = self.bca[0:1, 0:2048]
        lf = Sem(nc, self.lbl("ldwof"))
        self.wT(g)
        lf.inc(g.dma_start(aslo, self.W["AoT"][:]), 16)
        lf.inc(g.dma_start(bo_row, self.W["bor"][:]), 16)
        self.wV(s)
        self.wS(s)
        S.inc(s.activation(xbo_b, xbo_f[0:16, 0:2048], AF.Identity, scale=LORA_SCALE))
        self.wS(t)
        t.wait_ge(lf.h, lf.n)
        for ot in range(DTL):
            for tc in range(4):
                key = "C0" if tc % 2 == 0 else "C1"
                coff = (tc % 2) * 512
                self.pwait(t, key)
                t.matmul(P2C[:, coff:coff + 512], aslo[:, ot * 128:(ot + 1) * 128],
                         xbo_b[:, tc * 512:(tc + 1) * 512], start=True, stop=False)
                mm = t.matmul(P2C[:, coff:coff + 512],
                              bo_row[:, ot * 128:(ot + 1) * 128],
                              self.bca[0:1, 2048:2560], start=False, stop=True)
                T.inc(mm)
                v.wait_ge(T.h, T.n)
                V.inc(v.tensor_add(self.xs_t(ot)[:, tc * 512:(tc + 1) * 512],
                                   P2C[:, coff:coff + 512],
                                   self.xs_t(ot)[:, tc * 512:(tc + 1) * 512]))
                self.apb[key] = (V.h, V.n)

    def moe(self):
        nc = self.nc
        t, v, s, g, y = self.t, self.v, self.s, self.g, self.y
        T, V, S, G, Y = self.T, self.V, self.S, self.G, self.Y
        P2A, P2B, P2C, P1A, PBT = self.P2A, self.P2B, self.P2C, self.P1A, self.PBT
        wk, ar3 = self.ar2, self.ar3
        W = self.W
        ACT2, CWB, DSL = 0, 16384, 18432
        GSL, USL, RW3, SCR3 = 0, 4096, 8192, 8320

        lr = Sem(nc, self.lbl("ldrw"))
        self.wT(g)
        self.wS(g)
        self.wV(g)
        lr.inc(g.dma_start(ar3[:, RW3:RW3 + 128].rearrange("p (kt e) -> p kt e", kt=DTL),
                           W["rw"][:].rearrange("(kt p) e -> p kt e", p=128)), 16)
        t.wait_ge(lr.h, lr.n)
        cwT = self.bca[0:8, 0:2048]
        iota8 = self.misc[:, 132:148].bitcast(F32)
        sct = ar3[:, SCR3:SCR3 + 16].bitcast(F32)
        w1t = ar3[:, SCR3 + 16:SCR3 + 18].bitcast(F32)
        w2t = ar3[:, SCR3 + 18:SCR3 + 20].bitcast(F32)
        m1t = ar3[:, SCR3 + 20:SCR3 + 22].bitcast(F32)
        m2t = ar3[:, SCR3 + 22:SCR3 + 24].bitcast(F32)
        eqt = ar3[:, SCR3 + 24:SCR3 + 40].bitcast(F32)
        tt1 = ar3[:, SCR3 + 40:SCR3 + 56].bitcast(F32)
        i1t = ar3[:, SCR3 + 56:SCR3 + 58].bitcast(F32)
        f1t = ar3[:, SCR3 + 58:SCR3 + 74].bitcast(F32)
        s2t = ar3[:, SCR3 + 74:SCR3 + 90].bitcast(F32)
        f2t = ar3[:, SCR3 + 90:SCR3 + 106].bitcast(F32)
        cwf = ar3[:, SCR3 + 106:SCR3 + 122].bitcast(F32)
        et1 = ar3[:, SCR3 + 122:SCR3 + 126].bitcast(F32)
        cwp = ar3[:, SCR3 + 128:SCR3 + 256]
        sil = ar3[:, SCR3 + 256:SCR3 + 768]
        utm = ar3[:, SCR3 + 768:SCR3 + 1280]

        def vb(inst):
            V.inc(inst)
            self.wV(v)

        for blk in range(NBLK):
            bsl_ = slice(blk * 128, (blk + 1) * 128)
            self.wV(t)
            self.wS(t)
            for kt in range(DTL):
                mm = t.matmul(P1A[:, 0:8], self.lna_t(kt)[:, bsl_],
                              ar3[:, RW3 + kt * 8:RW3 + (kt + 1) * 8],
                              start=(kt == 0), stop=(kt == DTL - 1))
            T.inc(mm)
            self.wT(v)
            vb(v.tensor_copy(sct, P1A[:, 0:8]))
            vb(v.reduce_max(m1t[:, 0:1], sct, axis=AXX))
            vb(v.tensor_tensor(eqt, sct, m1t[:, 0:1].broadcast_to([128, 8]), ALU.is_equal))
            vb(v.scalar_tensor_tensor(tt1, eqt, -1000.0, iota8, ALU.mult, ALU.add))
            vb(v.tensor_reduce(i1t[:, 0:1], tt1, AXX, ALU.min))
            vb(v.tensor_tensor(f1t, tt1, i1t[:, 0:1].broadcast_to([128, 8]), ALU.is_equal))
            vb(v.scalar_tensor_tensor(s2t, f1t, -1e30, sct, ALU.mult, ALU.add))
            vb(v.reduce_max(m2t[:, 0:1], s2t, axis=AXX))
            vb(v.tensor_tensor(eqt, s2t, m2t[:, 0:1].broadcast_to([128, 8]), ALU.is_equal))
            vb(v.scalar_tensor_tensor(tt1, eqt, -1000.0, iota8, ALU.mult, ALU.add))
            vb(v.tensor_reduce(i1t[:, 0:1], tt1, AXX, ALU.min))
            vb(v.tensor_tensor(f2t, tt1, i1t[:, 0:1].broadcast_to([128, 8]), ALU.is_equal))
            vb(v.tensor_sub(et1[:, 0:1], m2t[:, 0:1], m1t[:, 0:1]))
            self.wV(s)
            S.inc(s.activation(et1[:, 1:2], et1[:, 0:1], AF.Exp))
            self.wS(v)
            vb(v.tensor_scalar_add(et1[:, 0:1], et1[:, 1:2], 1.0))
            vb(v.reciprocal(w1t[:, 0:1], et1[:, 0:1]))
            vb(v.tensor_mul(w2t[:, 0:1], et1[:, 1:2], w1t[:, 0:1]))
            vb(v.tensor_scalar_mul(cwf, f1t, w1t[:, 0:1]))
            vb(v.scalar_tensor_tensor(cwf, f2t, w2t[:, 0:1], cwf, ALU.mult, ALU.add))
            vb(v.tensor_copy(cwp[:, 0:8], cwf))
            self.wV(t)
            T.inc(t.transpose(PBT[:, 0:128], cwp, self.idn))
            self.wT(v)
            vb(v.tensor_copy(cwT[:, bsl_], PBT[0:8, 0:128]))

        lgu = Sem(nc, self.lbl("lgu"))
        ldn = Sem(nc, self.lbl("ldn"))
        buf_t_after = {0: 0, 1: 0}

        def load_gu(e, ft):
            bufp = ft % 2
            lgu.inc(y.dma_start(
                ar3[:, GSL + bufp * 2048:GSL + (bufp + 1) * 2048]
                .rearrange("p (kt c) -> p kt c", kt=DTL),
                W["gate"][e][:, ft * 128:(ft + 1) * 128]
                .rearrange("(kt p) c -> p kt c", p=128)), 16)
            lgu.inc(y.dma_start(
                ar3[:, USL + bufp * 2048:USL + (bufp + 1) * 2048]
                .rearrange("p (kt c) -> p kt c", kt=DTL),
                W["up"][e][:, ft * 128:(ft + 1) * 128]
                .rearrange("(kt p) c -> p kt c", p=128)), 16)
            return lgu.n

        self.wV(y)
        self.wT(y)
        self.dn_t_after = {0: 0, 1: 0}
        self.dn_ready = {}
        self.gu_vmark = {0: 0, 1: 0}
        self.gu_smark = {0: 0, 1: 0}
        gu_ready = {(0, 0): load_gu(0, 0)}
        for e in range(MOE_NE):
            self.wV(t)
            for tc in range(4):
                T.inc(t.matmul(P1A[0:1, 0:512], self.idn[0:8, e:e + 1],
                               cwT[:, tc * 512:(tc + 1) * 512], start=True, stop=True))
                self.wT(v)
                cwrb = self.rows[0:1, 0:256].bitcast(BF16)
                V.inc(v.tensor_copy(cwrb, P1A[0:1, 0:512]))
                self.wV(t)
                T.inc(t.matmul(P2C[:, (tc % 2) * 512:(tc % 2 + 1) * 512],
                               self.bca[0:1, 2048:2176], cwrb,
                               start=True, stop=True))
                self.wT(v)
                V.inc(v.tensor_copy(wk[:, CWB + tc * 512:CWB + (tc + 1) * 512],
                                   P2C[:, (tc % 2) * 512:(tc % 2 + 1) * 512]))
                self.wV(t)
            for ft in range(8):
                bufp = ft % 2
                t.wait_ge(lgu.h, gu_ready[(e, ft)])
                ne, nf = (e, ft + 1) if ft < 7 else (e + 1, 0)
                if ne < MOE_NE:
                    if buf_t_after[nf % 2]:
                        y.wait_ge(T.h, buf_t_after[nf % 2])
                    gu_ready[(ne, nf)] = load_gu(ne, nf)
                for tc in range(4):
                    ps = P2A if tc % 2 == 0 else P2B
                    sl = slice(tc * 512, (tc + 1) * 512)
                    bk = tc % 2
                    if self.gu_vmark[bk]:
                        t.wait_ge(V.h, self.gu_vmark[bk])
                        t.wait_ge(S.h, self.gu_smark[bk])
                    for kt in range(DTL):
                        t.matmul(ps[:, 0:512],
                                 ar3[:, GSL + bufp * 2048 + kt * 128:GSL + bufp * 2048 + (kt + 1) * 128],
                                 self.lna_t(kt)[:, sl], start=(kt == 0), stop=(kt == DTL - 1))
                    for kt in range(DTL):
                        mm = t.matmul(ps[:, 512:1024],
                                      ar3[:, USL + bufp * 2048 + kt * 128:USL + bufp * 2048 + (kt + 1) * 128],
                                      self.lna_t(kt)[:, sl], start=(kt == 0), stop=(kt == DTL - 1))
                    T.inc(mm)
                    self.wT(s)
                    self.wV(s)
                    S.inc(s.activation(sil, ps[:, 0:512], AF.Silu))
                    self.gu_smark[bk] = S.n
                    self.wT(v)
                    V.inc(v.tensor_mul(utm, ps[:, 512:1024],
                                       wk[:, CWB + tc * 512:CWB + (tc + 1) * 512]))
                    self.wS(v)
                    self.wV(v)
                    V.inc(v.tensor_mul(
                        wk[:, ACT2 + ft * 2048 + tc * 512:ACT2 + ft * 2048 + (tc + 1) * 512],
                        sil, utm))
                    self.gu_vmark[bk] = V.n
                buf_t_after[ft % 2] = T.n
            self.wV(t)

            def load_dn(ot_):
                g.wait_ge(T.h, self.dn_t_after[ot_ % 2])
                ldn.inc(g.dma_start(
                    wk[:, DSL + (ot_ % 2) * 1024:DSL + (ot_ % 2 + 1) * 1024]
                    .rearrange("p (kt c) -> p kt c", kt=8),
                    W["down"][e][ot_].rearrange("(kt p) c -> p kt c", p=128)), 16)
                self.dn_ready[ot_] = ldn.n
            load_dn(0)
            vm = {0: 0, 1: 0}
            for ot in range(DTL):
                bufp = ot % 2
                t.wait_ge(ldn.h, self.dn_ready[ot])
                if ot + 1 < DTL:
                    load_dn(ot + 1)
                for tc in range(4):
                    ps = P2C if tc % 2 == 0 else P1A
                    if vm[tc % 2]:
                        t.wait_ge(V.h, vm[tc % 2])
                    for kt in range(8):
                        mm = t.matmul(ps[:, 0:512],
                                      wk[:, DSL + bufp * 1024 + kt * 128:DSL + bufp * 1024 + (kt + 1) * 128],
                                      wk[:, ACT2 + kt * 2048 + tc * 512:ACT2 + kt * 2048 + (tc + 1) * 512],
                                      start=(kt == 0), stop=(kt == 7))
                    T.inc(mm)
                    self.wT(v)
                    V.inc(v.tensor_add(self.xs_t(ot)[:, tc * 512:(tc + 1) * 512],
                                       ps[:, 0:512],
                                       self.xs_t(ot)[:, tc * 512:(tc + 1) * 512]))
                    vm[tc % 2] = V.n
                self.dn_t_after[bufp] = T.n
            self.wV(t)

    def s6(self):
        nc = self.nc
        t, v, s, g, y = self.t, self.v, self.s, self.g, self.y
        T, V, S, G, Y = self.T, self.V, self.S, self.G, self.Y
        P2A, P2B, P2C, P1A, PBT = self.P2A, self.P2B, self.P2C, self.P1A, self.PBT
        wk, ar3 = self.ar2, self.ar3
        W = self.W
        self.own_y = nc.dram_tensor("own_y", [8 * 128, TOK], BF16, kind="Internal").ap()

        lb = Sem(nc, self.lbl("ls6b"))
        lb.inc(g.dma_start(self.acol[:, 260:276], W["dt_bias"][:].rearrange("e p one -> p (e one)"),
                           allow_slow_non_contiguous=True), 16)
        lb.inc(g.dma_start(self.acol[:, 276:292], W["obias"][:].rearrange("e p one -> p (e one)"),
                           allow_slow_non_contiguous=True), 16)
        lb.inc(g.dma_start(self.acol[:, 292:308], W["Dvec"][:].rearrange("e p one -> p (e one)"),
                           allow_slow_non_contiguous=True), 16)
        le, lo, ldn_ = self.lbl("laE"), self.lbl("laO"), self.lbl("laD")
        nc.br_eq(self.role_r, 0, le, lo, engines=POOL)
        asrc = W["A_log"][:]
        with nc.body(le, valid_engines=POOL):
            lb.inc(g.dma_start(
                self.acol[:, 0:128].rearrange("p (db n) -> p db n", db=8),
                asrc[0:1024, :].rearrange("(db p) n -> p db n", p=128)), 16)
            g.br(ldn_)
        with nc.body(lo, valid_engines=POOL):
            g.dma_start(
                self.acol[:, 0:128].rearrange("p (db n) -> p db n", db=8),
                asrc[1024:2048, :].rearrange("(db p) n -> p db n", p=128)).then_inc(lb.h, 16)
            g.br(ldn_)
        nc.switch_bb(ldn_)
        self.wG(s)
        s.wait_ge(lb.h, lb.n)
        S.inc(s.activation(self.acol[:, 0:128], self.acol[:, 0:128], AF.Exp, scale=1.0))
        self.wS(v)
        V.inc(v.tensor_scalar_mul(self.acol[:, 0:128], self.acol[:, 0:128], -1.0))
        self.wV(s)

        self._s6_pre()
        v.wait_ge(lb.h, lb.n)
        self.wV(v)
        for ot in range(DTL):
            V.inc(v.scalar_tensor_tensor(self.xs_t(ot), self.lna_t(ot),
                                         self.acol[:, 292 + ot:293 + ot],
                                         self.xs_t(ot), ALU.mult, ALU.add))
            self.wV(v)
        self._s6_scan()

    def _s6_pre(self):
        nc = self.nc
        t, v, s, g, y = self.t, self.v, self.s, self.g, self.y
        T, V, S, G = self.T, self.V, self.S, self.G
        P2A, P2B, P1A = self.P2A, self.P2B, self.P1A
        wk, ar3 = self.ar2, self.ar3
        W = self.W
        dxw = Sem(nc, "dxw")
        DTD, DTW, S1T, DLT, DXT = 0, 2048, 4096, 6144, 8192
        ld = Sem(nc, self.lbl("ls6p"))
        self.wT(g)
        self.wV(g)
        self.wS(g)
        ld.inc(g.dma_start(wk[:, DTD:DTD + 2048].rearrange("p (kt c) -> p kt c", kt=DTL),
                           W["dt_down"][:].rearrange("(kt p) c -> p kt c", p=128)), 16)
        ld.inc(g.dma_start(wk[:, DTW:DTW + 2048].rearrange("p (et c) -> p et c", et=DTL),
                           W["dtwT"][:].rearrange("et r c -> r et c")), 16)
        ld.inc(g.dma_start(ar3[:, 0:512].rearrange("p (kt c) -> p kt c", kt=DTL),
                           W["WBC"][:].rearrange("(kt p) c -> p kt c", p=128)), 16)
        t.wait_ge(ld.h, ld.n)
        for tc in range(4):
            ps = P2A if tc % 2 == 0 else P2B
            self.wS(t)
            for kt in range(DTL):
                mm = t.matmul(ps[:, 0:512], wk[:, DTD + kt * 128:DTD + (kt + 1) * 128],
                              self.lna_t(kt)[:, tc * 512:(tc + 1) * 512],
                              start=(kt == 0), stop=(kt == DTL - 1))
            for kt in range(DTL):
                mm2 = t.matmul(P1A[0:32, 0:512], ar3[:, kt * 32:(kt + 1) * 32],
                               self.lna_t(kt)[:, tc * 512:(tc + 1) * 512],
                               start=(kt == 0), stop=(kt == DTL - 1))
            T.inc(mm)
            T.inc(mm2)
            self.wT(s)
            S.inc(s.activation(wk[:, S1T + tc * 512:S1T + (tc + 1) * 512], ps[:, 0:512],
                               AF.Identity))
            self.wT(v)
            V.inc(v.tensor_copy(ar3[0:32, 512 + tc * 512:512 + (tc + 1) * 512],
                                P1A[0:32, 0:512]))
            self.wV(t)
            self.wS(t)
        self.wV(g)
        self.pair_write(lambda r: self.sh_bc[r * 32:r * 32 + 32, :],
                        ar3[0:32, 512:2560], dxw)
        for et in range(DTL):
            for tc in range(4):
                ps = P2A if tc % 2 == 0 else P2B
                self.wS(t)
                self.wV(t)
                mm = t.matmul(ps[:, 0:512], wk[:, DTW + et * 128:DTW + (et + 1) * 128],
                              wk[:, S1T + tc * 512:S1T + (tc + 1) * 512],
                              start=True, stop=True)
                T.inc(mm)
                self.wT(s)
                dtb = self.acol[:, 260 + et:261 + et]
                ab = ar3[:, 2560:3072]
                rp = ar3[:, 3072:3584]
                S.inc(s.activation(ab, ps[:, 0:512], AF.Abs, bias=dtb))
                S.inc(s.activation(rp, ps[:, 0:512], AF.Relu, bias=dtb))
                self.wS(s)
                S.inc(s.activation(ab, ab, AF.Exp, scale=-1.0))
                self.wS(s)
                S.inc(s.activation(ab, ab, AF.Ln, bias=1.0))
                self.wS(v)
                dsl_ = slice(DLT + tc * 512, DLT + (tc + 1) * 512)
                xsl_ = slice(DXT + tc * 512, DXT + (tc + 1) * 512)
                V.inc(v.tensor_add(wk[:, dsl_], rp, ab))
                self.wV(v)
                V.inc(v.tensor_mul(wk[:, xsl_], wk[:, dsl_],
                                   self.lna_t(et)[:, tc * 512:(tc + 1) * 512]))
                self.wV(t)
                self.wV(s)
            self.wV(g)
            self.pair_write(lambda r, et=et: self.sh_dx[(r * 2 + 0) * D + et * 128:
                                                        (r * 2 + 0) * D + (et + 1) * 128, :],
                            wk[:, DLT:DLT + 2048], dxw)
            self.pair_write(lambda r, et=et: self.sh_dx[(r * 2 + 1) * D + et * 128:
                                                        (r * 2 + 1) * D + (et + 1) * 128, :],
                            wk[:, DXT:DXT + 2048], dxw)
            g.wait_ge(dxw.h, dxw.n)
        self.send_flag()
        self.wait_flag()

    def _s6_scan(self):
        nc = self.nc
        t, v, s, g, y = self.t, self.v, self.s, self.g, self.y
        T, V, S, G = self.T, self.V, self.S, self.G
        P2C, P1A = self.P2C, self.P1A
        wk, ar3 = self.ar2, self.ar3
        yacc = self.ar1[:, 16384:24576]
        state = self.misc[:, 256:512].bitcast(F32)
        self.wS(v)
        self.wT(v)
        V.inc(v.memset(state, 0.0))
        V.inc(v.memset(yacc, 0.0))
        self.wV(v)
        stg = Sem(nc, self.lbl("stg"))
        svb = Sem(nc, "svb")
        sae = Sem(nc, "sae")
        self.po_lw = Sem(nc, "polw")
        self.po_lym = Sem(nc, "polym")
        self.po_lw_ready = {}
        self.po_lw_t_after = {0: self.T.n, 1: self.T.n}
        self.po_t_mark = {}
        self.po_ps_smark = {0: 0, 1: 0}
        self.po_ro_vmark = 0
        self.po_ymv_ready = 0
        self.po_ymv_t_after = self.T.n
        self._a_free = {0: 0, 1: 0}
        self._bc_smark = {0: 0, 1: 0}
        bufs = ((ar3[:, 8192:9216], ar3[:, 9216:10240]),
                (ar3[:, 5120:6144], ar3[:, 6144:7168]))
        bc_free = {0: 0, 1: 0}
        for q in range(4):
            self.wV(g)
            g.wait_ge(svb.h, svb.n)
            self.wS(g)
            for xi, dst0 in ((0, 0), (1, 8192)):
                for r in (0, 1):
                    le, lo, ldn_ = self.lbl("sgE"), self.lbl("sgO"), self.lbl("sgD")
                    nc.br_eq(self.role_r, 0, le, lo, engines=POOL)
                    for lbl, myh in ((le, 0), (lo, 1)):
                        with nc.body(lbl, valid_engines=POOL):
                            for db in range(8):
                                src = self.sh_dx[(r * 2 + xi) * D + myh * 1024 + db * 128:
                                                 (r * 2 + xi) * D + myh * 1024 + (db + 1) * 128,
                                                 4 * q * 128:(4 * q + 4) * 128]
                                src3 = src.rearrange("p (j c) -> p j c", c=128)
                                dst = self.ar1[:, dst0 + db * 1024 + r * 128:
                                               dst0 + db * 1024 + r * 128 + 896]
                                dst3 = bass.AP(dst.tensor, dst.offset,
                                               [[32768, 128], [256, 4], [1, 128]])
                                inst = g.dma_start(dst3, src3)
                                if myh == 0:
                                    stg.inc(inst, 16)
                                else:
                                    inst.then_inc(stg.h, 16)
                            g.br(ldn_)
                    nc.switch_bb(ldn_)
            for r in (0, 1):
                src = self.sh_bc[r * 32:r * 32 + 32, 4 * q * 128:(4 * q + 4) * 128]
                src3 = src.rearrange("w (j c) -> w j c", c=128)
                dst = ar3[0:32, r * 128:r * 128 + 896]
                dst3 = bass.AP(dst.tensor, dst.offset, [[11264, 32], [256, 4], [1, 128]])
                stg.inc(g.dma_start(dst3, src3), 16)
            if q >= 1:
                self._post_ymov(q - 1)
            v.wait_ge(stg.h, stg.n)
            t.wait_ge(stg.h, stg.n)
            s.wait_ge(stg.h, stg.n)
            self._bc_bcast(0, bufs[0], bc_free[0], svb, 0)
            for n in range(NST):
                if n + 1 < NST:
                    self._bc_bcast(n + 1, bufs[(n + 1) % 2], bc_free[(n + 1) % 2], svb,
                                   (n + 1) % 2)
                if q >= 1:
                    if n >= 6 and n - 6 <= 9:
                        self._post_SR(q - 1, n - 6)
                    if n >= 4 and n - 4 <= 11:
                        self._post_T(n - 4)
                Bb, Cb = bufs[n % 2]
                v.wait_ge(self.S.h, self._bc_smark[n % 2])
                for db in range(8):
                    col = db * NST + n
                    a_t = wk[:, 0:1024] if db % 2 == 0 else wk[:, 16384:17408]
                    u_t = wk[:, 1024:2048]
                    h_t = wk[:, 2048:3072]
                    z_t = wk[:, 3072:4096]
                    s.wait_ge(svb.h, self._a_free[db % 2])
                    sae.inc(s.activation(a_t, self.ar1[:, db * 1024:(db + 1) * 1024],
                                         AF.Exp, scale=self.acol[:, col:col + 1]))
                    svb.inc(v.tensor_mul(u_t, self.ar1[:, 8192 + db * 1024:8192 + (db + 1) * 1024], Bb))
                    v.wait_ge(sae.h, sae.n)
                    v.wait_ge(svb.h, svb.n)
                    svb.inc(v.tensor_tensor_scan(h_t, a_t, u_t, state[:, col:col + 1],
                                                 ALU.mult, ALU.add))
                    self._a_free[db % 2] = svb.n
                    v.wait_ge(svb.h, svb.n)
                    svb.inc(v.tensor_copy(state[:, col:col + 1], h_t[:, 1023:1024]))
                    svb.inc(v.tensor_mul(z_t, h_t, Cb))
                    v.wait_ge(svb.h, svb.n)
                    svb.inc(v.tensor_add(yacc[:, db * 1024:(db + 1) * 1024],
                                         yacc[:, db * 1024:(db + 1) * 1024], z_t))
                    v.wait_ge(svb.h, svb.n)
                bc_free[n % 2] = svb.n
            if q >= 1:
                for i in range(10, 16):
                    self._post_SR(q - 1, i)
                    if i + 2 < 16:
                        self._post_T(i + 2)
            self.wV(g)
            g.wait_ge(svb.h, svb.n)
            self.wV(t)
            yq = Sem(nc, self.lbl("yq"))
            for db in range(8):
                ybase = yacc[:, db * 1024:(db + 1) * 1024]
                le, lo, ldn_ = self.lbl("yoE"), self.lbl("yoO"), self.lbl("yoD")
                nc.br_eq(self.role_r, 0, le, lo, engines=POOL)
                for lbl, myh in ((le, 0), (lo, 1)):
                    with nc.body(lbl, valid_engines=POOL):
                        i1 = g.dma_start(
                            self.own_y[db * 128:(db + 1) * 128, 4 * q * 128:(4 * q + 4) * 128]
                            .rearrange("p (j c) -> p j c", c=128),
                            bass.AP(ybase.tensor, ybase[0:1, myh * 128:myh * 128 + 1].offset,
                                    [[32768, 128], [256, 4], [1, 128]]))
                        i2 = g.dma_start(
                            self.sh_y[(myh * 8 + db) * 128:(myh * 8 + db + 1) * 128,
                                      4 * q * 128:(4 * q + 4) * 128]
                            .rearrange("p (j c) -> p j c", c=128),
                            bass.AP(ybase.tensor,
                                    ybase[0:1, (1 - myh) * 128:(1 - myh) * 128 + 1].offset,
                                    [[32768, 128], [256, 4], [1, 128]]))
                        if myh == 0:
                            yq.inc(i1, 16)
                            yq.inc(i2, 16)
                        else:
                            i1.then_inc(yq.h, 16)
                            i2.then_inc(yq.h, 16)
                        g.br(ldn_)
                nc.switch_bb(ldn_)
            g.wait_ge(yq.h, yq.n)
            self.send_flag()
            if q < 3:
                v.wait_ge(yq.h, yq.n)
                V.inc(v.memset(yacc, 0.0))
                self.wV(v)
        self._post_ymov(3)
        self._post_T(0)
        for ot in range(DTL):
            if ot + 1 < DTL:
                self._post_T(ot + 1)
            self._post_SR(3, ot)

    def _bc_bcast(self, n, buf, free_mark, svb, bufidx):
        t, s = self.t, self.s
        T, S = self.T, self.S
        ar3, P1A, P2C = self.ar3, self.P1A, self.P2C
        t.wait_ge(svb.h, free_mark)
        bcrb = self.rows[0:1, 0:256].bitcast(BF16)
        for ri, dst in ((n, buf[0]), (16 + n, buf[1])):
            for c2 in range(2):
                t.wait_ge(S.h, S.n)
                T.inc(t.matmul(P1A[0:1, 0:512], self.idn[0:32, ri:ri + 1],
                               ar3[0:32, c2 * 512:(c2 + 1) * 512],
                               start=True, stop=True))
                s.wait_ge(T.h, T.n)
                S.inc(s.activation(bcrb, P1A[0:1, 0:512], AF.Identity))
                t.wait_ge(S.h, S.n)
                T.inc(t.matmul(P2C[:, c2 * 512:(c2 + 1) * 512], self.bca[0:1, 2048:2176],
                               bcrb, start=True, stop=True))
                s.wait_ge(T.h, T.n)
                S.inc(s.activation(dst[:, c2 * 512:(c2 + 1) * 512],
                                   P2C[:, c2 * 512:(c2 + 1) * 512], AF.Identity))
        self._bc_smark[bufidx] = S.n

    def _post_ymov(self, qq):
        nc, g = self.nc, self.g
        wk = self.ar2
        self.wait_flag()
        if self.po_ymv_t_after:
            g.wait_ge(self.T.h, self.po_ymv_t_after)
        lym = self.po_lym
        le, lo, ldn_ = self.lbl("ymE"), self.lbl("ymO"), self.lbl("ymD")
        nc.br_eq(self.role_r, 0, le, lo, engines=POOL)
        for lbl, myh in ((le, 0), (lo, 1)):
            with nc.body(lbl, valid_engines=POOL):
                i1 = g.dma_start(
                    wk[:, 4096 + myh * 4096:4096 + myh * 4096 + 4096]
                    .rearrange("p (db c) -> p db c", db=8),
                    self.own_y[:, qq * 512:(qq + 1) * 512]
                    .rearrange("(db p) c -> p db c", p=128))
                i2 = g.dma_start(
                    wk[:, 4096 + (1 - myh) * 4096:4096 + (1 - myh) * 4096 + 4096]
                    .rearrange("p (db c) -> p db c", db=8),
                    self.sh_y[(1 - myh) * 8 * 128:((1 - myh) * 8 + 8) * 128,
                              qq * 512:(qq + 1) * 512]
                    .rearrange("(db p) c -> p db c", p=128))
                if myh == 0:
                    lym.inc(i1, 16)
                    lym.inc(i2, 16)
                else:
                    i1.then_inc(lym.h, 16)
                    i2.then_inc(lym.h, 16)
                g.br(ldn_)
        nc.switch_bb(ldn_)
        self.po_ymv_ready = lym.n
        self._load_wsl(0)

    def _load_wsl(self, ot):
        y = self.y
        wk = self.ar2
        buf = ot % 2
        if self.po_lw_t_after[buf]:
            y.wait_ge(self.T.h, self.po_lw_t_after[buf])
        self.po_lw.inc(y.dma_start(
            wk[:, 12288 + buf * 2048:12288 + (buf + 1) * 2048]
            .rearrange("p (kt c) -> p kt c", kt=DTL),
            self.W["opwT"][ot].rearrange("(kt p) c -> p kt c", p=128)), 16)
        self.po_lw_ready[ot] = self.po_lw.n

    def _post_T(self, ot):
        t = self.t
        T, S = self.T, self.S
        wk = self.ar2
        ps = self.P2A if ot % 2 == 0 else self.P2B
        t.wait_ge(self.po_lym.h, self.po_ymv_ready)
        t.wait_ge(self.po_lw.h, self.po_lw_ready[ot])
        if ot + 1 < DTL:
            self._load_wsl(ot + 1)
        buf = ot % 2
        if self.po_ps_smark[ot % 2]:
            t.wait_ge(S.h, self.po_ps_smark[ot % 2])
        for kt in range(DTL):
            mm = t.matmul(ps[:, 0:512],
                          wk[:, 12288 + buf * 2048 + kt * 128:12288 + buf * 2048 + (kt + 1) * 128],
                          wk[:, 4096 + kt * 512:4096 + (kt + 1) * 512],
                          start=(kt == 0), stop=(kt == DTL - 1))
        T.inc(mm)
        self.po_lw_t_after[buf] = T.n
        self.po_t_mark[ot] = T.n
        if ot == DTL - 1:
            self.po_ymv_t_after = T.n

    def _post_SR(self, qq, ot):
        v, s = self.v, self.s
        V, S = self.V, self.S
        ps = self.P2A if ot % 2 == 0 else self.P2B
        ro = self.ar3[:, 2560:3072]
        s.wait_ge(self.T.h, self.po_t_mark[ot])
        if self.po_ro_vmark:
            s.wait_ge(V.h, self.po_ro_vmark)
        S.inc(s.activation(ro, ps[:, 0:512], AF.Relu,
                           bias=self.acol[:, 276 + ot:277 + ot]))
        self.po_ps_smark[ot % 2] = S.n
        v.wait_ge(S.h, S.n)
        xsl = self.xs_t(ot)[:, qq * 512:(qq + 1) * 512]
        V.inc(v.tensor_add(xsl, ro, xsl))
        self.po_ro_vmark = V.n


def build_nc(detect_races=True):
    nc = bass.Bass("TRN2", target_bir_lowering=False, debug=False,
                   detect_race_conditions=detect_races)
    Prog(nc)
    return nc


def kernel(**inputs):
    nonce = int(np.random.randint(1 << 20, 1 << 29))
    nc = build_nc()
    in_maps = [make_inputs(c, inputs, nonce) for c in range(8)]
    res = run_bass_kernel_spmd(nc, in_maps, core_ids=list(range(8)))
    return reassemble([r for r in res.results])


# version 13
# speedup vs baseline: 1.0474x; 1.0474x over previous
import numpy as np
import ml_dtypes

import concourse.bass as bass
import concourse.mybir as mybir
from concourse.bass_utils import run_bass_kernel_spmd

F32 = mybir.dt.float32
BF16 = mybir.dt.bfloat16
I32 = mybir.dt.int32
AF = mybir.ActivationFunctionType
ALU = mybir.AluOpType
AXX = mybir.AxisListType.X
PoolE = mybir.EngineType.Pool
POOL = [PoolE]

B, L, D = 4, 4096, 2048
HEADS, HD = 16, 128
RANK = 16
LORA_SCALE = 0.5
ATT_SCALE = float(HD) ** -0.5
EXP_SHIFT = -16.0
DT_RANK, NST = 128, 16
NEXP, FF = 8, 1024
TOK = 2048
NBLK = TOK // 128
DTL = D // 128
LN_EPS = 1e-5

N_STAGES = 4
STOP_AT = 99
WO_PARTS = 7
MOE_NE = NEXP



def _bf(a):
    return np.ascontiguousarray(np.asarray(a)).astype(ml_dtypes.bfloat16)


def _f32(a):
    return np.ascontiguousarray(np.asarray(a), dtype=np.float32)


def _tiled_T(w):
    w = np.asarray(w)
    return np.ascontiguousarray(w.reshape(w.shape[0] // 128, 128, w.shape[1]).transpose(0, 2, 1))


def make_inputs(core, full, nonce):
    b, par = core // 2, core & 1
    x = np.asarray(full["L0_attn_a_x"])
    blocks = [x[b, gg * 128:(gg + 1) * 128, :] for gg in range(par, 2 * NBLK, 2)]
    xc = np.concatenate(blocks, axis=0)
    d = {
        "x_in": _f32(xc.T),
        "role": np.array([[par, 0]], np.int32),
        "flagv": (nonce + 1 + np.arange(64, dtype=np.int64)).astype(np.int32).reshape(1, 64),
    }
    for nm in ("q", "k", "v", "o"):
        d[f"W{nm}T"] = _bf(_tiled_T(full[f"L0_attn_a_W{nm}"]))
        d[f"B{nm}T"] = _bf(np.asarray(full[f"L0_attn_a_B{nm}"]).T)
        d[f"A{nm}T"] = _bf(np.asarray(full[f"L0_attn_a_A{nm}"]).T)
        d[f"b{nm}r"] = _bf(np.asarray(full[f"L0_attn_a_b{nm}"]).reshape(1, D))
    d["rw"] = _bf(full["L0_ffn_router_w"])
    d["gate"] = _bf(full["L0_ffn_gate_proj"])
    d["up"] = _bf(full["L0_ffn_up_proj"])
    d["down"] = _bf(np.stack([_tiled_T(np.asarray(full["L0_ffn_down_proj"])[e].T)
                              for e in range(NEXP)]))
    xpwT = np.asarray(full["L0_attn_b_x_proj_weight"]).T
    d["dt_down"] = _bf(xpwT[:, 0:DT_RANK])
    d["WBC"] = _bf(xpwT[:, DT_RANK:DT_RANK + 2 * NST])
    d["dtwT"] = _bf(_tiled_T(full["L0_attn_b_dt_proj_weight"]))
    d["dt_bias"] = _f32(np.asarray(full["L0_attn_b_dt_proj_bias"]).reshape(DTL, 128, 1))
    d["A_log"] = _f32(full["L0_attn_b_A_log"])
    d["Dvec"] = _f32(np.asarray(full["L0_attn_b_D"]).reshape(DTL, 128, 1))
    d["opwT"] = _bf(_tiled_T(full["L0_attn_b_out_proj_weight"]))
    d["obias"] = _f32(np.asarray(full["L0_attn_b_out_proj_bias"]).reshape(DTL, 128, 1))
    return d


def reassemble(outs):
    yy = np.zeros((B, L, D), np.float32)
    for c in range(8):
        b, par = c // 2, c & 1
        yc = np.asarray(outs[c]["y_out"]).T
        for i, gg in enumerate(range(par, 2 * NBLK, 2)):
            yy[b, gg * 128:(gg + 1) * 128, :] = yc[i * 128:(i + 1) * 128, :]
    return yy



class Sem:
    __slots__ = ("h", "n")

    def __init__(self, nc, name):
        self.h = nc.alloc_semaphore(name)
        self.n = 0

    def inc(self, inst, k=1):
        inst.then_inc(self.h, k)
        self.n += k
        return inst


class Prog:
    def __init__(self, nc):
        self.nc = nc
        self.t, self.v, self.s, self.g, self.y = (nc.tensor, nc.vector, nc.scalar,
                                                  nc.gpsimd, nc.sync)
        self.T = Sem(nc, "semT")
        self.V = Sem(nc, "semV")
        self.S = Sem(nc, "semS")
        self.G = Sem(nc, "semG")
        self.Y = Sem(nc, "semY")
        self._lbl = 0
        self.build()

    def lbl(self, base):
        self._lbl += 1
        return f"{base}{self._lbl}"

    def wT(self, eng):
        eng.wait_ge(self.T.h, self.T.n)

    def wV(self, eng):
        eng.wait_ge(self.V.h, self.V.n)

    def wS(self, eng):
        eng.wait_ge(self.S.h, self.S.n)

    def wG(self, eng):
        eng.wait_ge(self.G.h, self.G.n)

    def build(self):
        nc = self.nc
        t, v, s, g = self.t, self.v, self.s, self.g
        T, V, S, G = self.T, self.V, self.S, self.G

        def din(name, shape, dt_):
            return nc.dram_tensor(name, shape, dt_, kind="ExternalInput").ap()

        self.xin = din("x_in", [D, TOK], F32)
        self.role_in = din("role", [1, 2], I32)
        self.flagv_in = din("flagv", [1, 64], I32)
        self.yout = nc.dram_tensor("y_out", [D, TOK], F32, kind="ExternalOutput").ap()

        self.W = {}
        for nm, shape, dt_ in [
            ("WqT", [DTL, D, 128], BF16), ("WkT", [DTL, D, 128], BF16),
            ("WvT", [DTL, D, 128], BF16), ("WoT", [DTL, D, 128], BF16),
            ("BqT", [D, RANK], BF16), ("BkT", [D, RANK], BF16),
            ("BvT", [D, RANK], BF16), ("BoT", [D, RANK], BF16),
            ("AqT", [RANK, D], BF16), ("AkT", [RANK, D], BF16),
            ("AvT", [RANK, D], BF16), ("AoT", [RANK, D], BF16),
            ("bqr", [1, D], BF16), ("bkr", [1, D], BF16),
            ("bvr", [1, D], BF16), ("bor", [1, D], BF16),
            ("rw", [D, NEXP], BF16),
            ("gate", [NEXP, D, FF], BF16), ("up", [NEXP, D, FF], BF16),
            ("down", [NEXP, DTL, FF, 128], BF16),
            ("dt_down", [D, DT_RANK], BF16), ("WBC", [D, 2 * NST], BF16),
            ("dtwT", [DTL, DT_RANK, 128], BF16), ("dt_bias", [DTL, 128, 1], F32),
            ("A_log", [D, NST], F32), ("Dvec", [DTL, 128, 1], F32),
            ("opwT", [DTL, D, 128], BF16), ("obias", [DTL, 128, 1], F32),
        ]:
            self.W[nm] = din(nm, shape, dt_)

        def shr(name, rows):
            return nc.dram_tensor(name, [rows, TOK], BF16, kind="Internal",
                                  addr_space="Shared").ap()

        self.sh_kv = shr("sh_kv", 2 * 2 * HEADS * 128)
        self.sh_dx = shr("sh_dx", 2 * 2 * DTL * 128)
        self.sh_bc = shr("sh_bc", 2 * 2 * NST)
        self.sh_y = shr("sh_y", 2 * 8 * 128)
        self.sh_fl = nc.dram_tensor("sh_fl", [2, 64], I32, kind="Internal",
                                    addr_space="Shared").ap()

        A = nc.alloc_sbuf_tensor
        self.xs = A("xs", [128, DTL * TOK], BF16)
        self.ar1 = A("ar1", [128, DTL * TOK], BF16)
        self.ar2 = A("ar2", [128, 20480], BF16)
        self.ar3 = A("ar3", [128, 11264], BF16)
        self.rows = A("rows", [128, 1024], F32)
        self.rowsb = A("rowsb", [128, 2048], BF16)
        self.bca = A("bca", [128, 2560], BF16)
        self.rbv = self.ar2[:, 0:2048]
        self.nmb = self.ar2[:, 2048:4096]
        self.x2t = self.ar2[:, 4096:6144]
        self.misc = A("misc", [128, 512], BF16)
        self.role_t = A("role_t", [1, 2], I32)
        self.flag_t = A("flag_t", [1, 64], I32)
        self.fpoll = A("fpoll", [1, 2], I32)
        self.acol = A("acol", [128, 320], F32)

        self.P2A = nc.alloc_psum_tensor("P2A", [128, 1024], F32)
        self.P2B = nc.alloc_psum_tensor("P2B", [128, 1024], F32)
        self.P2C = nc.alloc_psum_tensor("P2C", [128, 1024], F32)
        self.P1A = nc.alloc_psum_tensor("P1A", [128, 512], F32)
        self.PBT = nc.alloc_psum_tensor("PBT", [128, 1024], BF16)

        self.idn = self.misc[:, 0:128]
        self.ones_c = self.misc[:, 128:130]
        self.zo_c = self.misc[:, 130:132]
        self.on33 = self.misc[:, 160:193]
        self.zz33 = self.misc[:, 193:226]
        self.iotaf = self.misc[:, 132:148].bitcast(F32)
        self.onesr = None

        riota = self.rows[:, 0:128]
        g.iota(riota, [[1, 128]], channel_multiplier=-1, allow_small_or_imprecise_dtypes=True)
        g.iota(self.iotaf, [[1, 8]], channel_multiplier=0, allow_small_or_imprecise_dtypes=True)
        g.memset(self.ones_c[:, 0:1], 1.0)
        g.memset(self.ones_c[:, 1:2], 0.0)
        g.memset(self.zo_c[:, 0:1], 0.0)
        g.memset(self.zo_c[:, 1:2], 1.0)
        g.memset(self.on33[:, 1:33], 0.0)
        g.memset(self.on33[:, 0:1], 1.0)
        g.memset(self.zz33[:, 0:32], 0.0)
        g.memset(self.zz33[:, 32:33], 1.0)
        self.onesb = self.bca[0:1, 2048:2560]
        for bb in (0, 32, 64, 96):
            g.memset(self.bca[bb:bb + 1, 2048:2560], 1.0)
            g.memset(self.acol[bb:bb + 1, 128:256], 1.0)
        g.memset(self.acol[:, 256:257], LN_EPS)
        g.memset(self.acol[:, 257:258], EXP_SHIFT)
        self.CSET = Sem(nc, "cset")
        self.CSET.inc(g.memset(self.acol[:, 258:259], 1.0))
        nc.const_aps.register(F32, LN_EPS, self.acol[:, 256:257])
        nc.const_aps.register(F32, EXP_SHIFT, self.acol[:, 257:258])
        nc.const_aps.register(F32, 1.0, self.acol[:, 258:259])
        G.inc(g.dma_start(self.role_t[:], self.role_in[:]), 16)
        G.inc(g.dma_start(self.flag_t[:], self.flagv_in[:]), 16)
        self.wG(v)
        v.wait_ge(self.CSET.h, self.CSET.n)
        V.inc(v.tensor_single_scalar(self.idn, riota, 0.0, ALU.is_equal))
        self.wG(g)
        self.wG(t)
        self.wG(s)
        t.wait_ge(self.CSET.h, self.CSET.n)
        s.wait_ge(self.CSET.h, self.CSET.n)
        g.wait_ge(self.CSET.h, self.CSET.n)

        self.role_r = nc.alloc_registers("role_r", engines=POOL)
        nc.reg_load(self.role_r[PoolE], self.role_t[:, 0:1])
        self.exp_r = nc.alloc_registers("exp_r", engines=POOL)
        self.f_r = nc.alloc_registers("f_r", engines=POOL)
        self.pw_r = nc.alloc_registers("pw_r", engines=POOL)
        nc.regs_mov(self.pw_r, 0)
        self.psem = Sem(nc, "psem")
        self.ev_sent = 0
        self.ev_wait = 0

        ldx = Sem(nc, "ldx")
        ldx_marks = []
        for ci in range(4):
            for d in range(DTL):
                ldx.inc(g.dma_start(self.xs_t(d)[:, ci * 512:(ci + 1) * 512],
                                    self.xin[d * 128:(d + 1) * 128, ci * 512:(ci + 1) * 512]), 16)
            ldx_marks.append(ldx.n)
        self.wG(t)
        self.wG(v)
        self.wG(s)

        if N_STAGES >= 1:
            self.layernorm(ld=(ldx, ldx_marks))
            if STOP_AT >= 2:
                self.attn_a()
        if N_STAGES >= 2:
            self.layernorm()
            self.moe()
        if N_STAGES >= 3:
            self.layernorm()
            self.s6()
        stw = Sem(nc, "stw")
        stored = set()

        def store_tile(ot, vmark):
            g.wait_ge(V.h, vmark)
            stw.inc(g.dma_start(self.yout[ot * 128:(ot + 1) * 128, :], self.xs_t(ot)), 16)
            stored.add(ot)

        if N_STAGES >= 4:
            self.layernorm()
            self.moe(store_cb=store_tile)

        self.wV(g)
        self.wT(g)
        self.wS(g)
        for d in range(DTL):
            if d not in stored:
                stw.inc(g.dma_start(self.yout[d * 128:(d + 1) * 128, :], self.xs_t(d)), 16)
        g.wait_ge(stw.h, stw.n)

    def xs_t(self, d):
        return self.xs[:, d * TOK:(d + 1) * TOK]

    def lna_t(self, d):
        return self.ar1[:, d * TOK:(d + 1) * TOK]

    def pair_write(self, dst_fn, src_ap, sem):
        nc, g = self.nc, self.g
        le, lo, ldn = self.lbl("pwE"), self.lbl("pwO"), self.lbl("pwD")
        nc.br_eq(self.role_r, 0, le, lo, engines=POOL)
        with nc.body(le, valid_engines=POOL):
            sem.inc(g.dma_start(dst_fn(0), src_ap), 16)
            g.br(ldn)
        with nc.body(lo, valid_engines=POOL):
            g.dma_start(dst_fn(1), src_ap).then_inc(sem.h, 16)
            g.br(ldn)
        nc.switch_bb(ldn)

    def send_flag(self):
        g = self.g
        ev = self.ev_sent
        self.ev_sent += 1
        self.pair_write(lambda r: self.sh_fl[r:r + 1, ev:ev + 1],
                        self.flag_t[:, ev:ev + 1], self.G)
        g.wait_ge(self.G.h, self.G.n)

    def wait_flag(self):
        nc, g = self.nc, self.g
        ev = self.ev_wait
        self.ev_wait += 1
        nc.reg_load(self.exp_r[PoolE], self.flag_t[:, ev:ev + 1])
        le, lo, ldn = self.lbl("plE"), self.lbl("plO"), self.lbl("plD")
        nc.br_eq(self.role_r, 0, le, lo, engines=POOL)
        for lbl, slot in ((le, 1), (lo, 0)):
            with nc.body(lbl, valid_engines=POOL):
                nc.regs_alu(self.pw_r, self.pw_r, 16, op=ALU.add)
                g.dma_start(self.fpoll[:, 0:1],
                            self.sh_fl[slot:slot + 1, ev:ev + 1]).then_inc(self.psem.h, 16)
                g.wait_ge(self.psem.h, self.pw_r)
                nc.reg_load(self.f_r[PoolE], self.fpoll[:, 0:1])
                nc.br_cmp(self.f_r, self.exp_r, lbl, lbl + "x", "IS_NE", engines=POOL)
            with nc.body(lbl + "x", valid_engines=POOL):
                g.br(ldn)
        nc.switch_bb(ldn)

    def layernorm(self, ld=None):
        t, v, s = self.t, self.v, self.s
        T, V, S = self.T, self.V, self.S
        P2A, P2B = self.P2A, self.P2B
        onesb16 = self.bca[0:1, 2048:2176]
        nrow = self.rows[0:1, 0:512]
        rrow = self.rows[0:1, 512:1024]
        nrow_b = self.ar3[0:1, 0:512]
        rrow_b = self.ar3[0:1, 512:1024]
        self.wV(t)
        self.wS(t)
        for ci in range(4):
            csl = slice(ci * 512, (ci + 1) * 512)
            if ld is not None:
                t.wait_ge(ld[0].h, ld[1][ci])
                v.wait_ge(ld[0].h, ld[1][ci])
            for d in range(DTL):
                t.matmul(P2A[0:1, 0:512], self.ones_c[:, 0:1], self.xs_t(d)[:, csl],
                         start=(d == 0), stop=(d == DTL - 1))
            for d in range(DTL):
                self.wT(v)
                V.inc(v.tensor_mul(self.x2t[:, 0:512], self.xs_t(d)[:, csl],
                                   self.xs_t(d)[:, csl]))
                self.wV(t)
                T.inc(t.matmul(P2A[0:1, 512:1024], self.ones_c[:, 0:1], self.x2t[:, 0:512],
                               start=(d == 0), stop=(d == DTL - 1)))
            self.wT(v)
            self.wS(v)
            V.inc(v.tensor_copy(nrow, P2A[0:1, 0:512]))
            V.inc(v.tensor_copy(rrow, P2A[0:1, 512:1024]))
            self.wV(s)
            S.inc(s.activation(nrow, nrow, AF.Identity, scale=-1.0 / D))
            self.wS(s)
            S.inc(s.activation(nrow_b, nrow, AF.Identity))
            self.wS(v)
            m2 = self.ar3[0:1, 1024:2048].bitcast(F32)
            V.inc(v.tensor_mul(m2, nrow, nrow))
            self.wV(v)
            V.inc(v.scalar_tensor_tensor(rrow, rrow, 1.0 / D, m2, ALU.mult, ALU.subtract))
            self.wV(s)
            S.inc(s.activation(rrow, rrow, AF.Ln, bias=LN_EPS))
            self.wS(s)
            S.inc(s.activation(rrow_b, rrow, AF.Exp, scale=-0.5))
            self.wS(t)
            self.wV(t)
            T.inc(t.matmul(P2B[:, 0:512], onesb16, nrow_b, start=True, stop=True))
            T.inc(t.matmul(P2B[:, 512:1024], onesb16, rrow_b, start=True, stop=True))
            self.wT(v)
            V.inc(v.tensor_copy(self.nmb[:, csl], P2B[:, 0:512]))
            V.inc(v.tensor_copy(self.rbv[:, csl], P2B[:, 512:1024]))
            self.wV(t)
        self.wV(v)
        for d in range(DTL):
            V.inc(v.tensor_add(self.lna_t(d), self.xs_t(d), self.nmb[:, :]))
            self.wV(v)
            V.inc(v.tensor_mul(self.lna_t(d), self.lna_t(d), self.rbv[:, :]))
            self.wV(v)

    def attn_a(self):
        nc = self.nc
        t, v, s, g = self.t, self.v, self.s, self.g
        T, V, S, G = self.T, self.V, self.S, self.G
        P2A, P2B, P2C, P1A, PBT = self.P2A, self.P2B, self.P2C, self.P1A, self.PBT
        W = self.W
        wk = self.ar2
        ar3 = self.ar3
        ATTO, KF, VTM, QH, WS = 0, 4096, 8192, 12288, 14336
        ld = Sem(nc, "aald")
        self.wT(g)
        self.wV(g)
        slots3 = {"q": 0, "k": 32, "v": 64}
        bias_rows = {}
        for nm, bb in (("q", 0), ("k", 32), ("v", 64)):
            dst = self.bca[bb:bb + 1, 0:2048]
            ld.inc(g.dma_start(dst, W[f"b{nm}r"][:]), 16)
            bias_rows[nm] = dst
        bsl = {}
        for i, nm in enumerate(("q", "k", "v", "o")):
            dst = ar3[:, i * 256:(i + 1) * 256]
            ld.inc(g.dma_start(dst.rearrange("p (kt r) -> p kt r", kt=DTL),
                               W[f"B{nm}T"][:].rearrange("(kt p) r -> p kt r", p=128)), 16)
            bsl[nm] = dst
        asl = {}
        for nm, bb in (("q", 0), ("k", 32), ("v", 64)):
            dst = ar3[bb:bb + 16, 1024:3072]
            ld.inc(g.dma_start(dst, W[f"A{nm}T"][:]), 16)
            asl[nm] = dst
        t.wait_ge(ld.h, ld.n)
        s.wait_ge(ld.h, ld.n)
        self.wV(t)
        if STOP_AT < 3:
            return

        xbr = {}
        slots = {"q": 0, "k": 32, "v": 64}
        for nm in ("q", "k", "v"):
            for tc in range(4):
                for kt in range(DTL):
                    mm = t.matmul(P1A[0:16, 0:512], bsl[nm][:, kt * 16:(kt + 1) * 16],
                                  self.lna_t(kt)[:, tc * 512:(tc + 1) * 512],
                                  start=(kt == 0), stop=(kt == DTL - 1))
                T.inc(mm)
                self.wT(s)
                dst = self.rowsb[slots[nm]:slots[nm] + 16, tc * 512:(tc + 1) * 512]
                S.inc(s.activation(dst, P1A[0:16, 0:512], AF.Identity, scale=LORA_SCALE))
                self.wS(t)
            xbr[nm] = self.rowsb[slots[nm]:slots[nm] + 16, 0:2048]

        def proj(psum, wsl_ap, xb_rows, bias_slice, aslc, tc, base):
            for kt in range(DTL):
                t.matmul(psum[:, 0:512], wsl_ap[:, kt * 128:(kt + 1) * 128],
                         self.lna_t(kt)[:, tc * 512:(tc + 1) * 512],
                         start=(kt == 0), stop=False)
            t.matmul(psum[:, 0:512], bias_slice,
                     self.bca[base:base + 1, 2048:2560], start=False, stop=False)
            return t.matmul(psum[:, 0:512], aslc, xb_rows[:, tc * 512:(tc + 1) * 512],
                            start=False, stop=True)

        self.apb = {}
        kvw = Sem(nc, "kvw")
        lkv = Sem(nc, self.lbl("ldkv"))
        kv_ready = {}
        kv_t_after = {0: T.n, 1: T.n}
        WSB = {0: WS, 1: KF}
        y = self.y
        KVE = y

        def load_kv(h):
            slot = h % 2
            if kv_t_after[slot]:
                KVE.wait_ge(T.h, kv_t_after[slot])
            base = WSB[slot]
            lkv.inc(KVE.dma_start(wk[:, base:base + 2048].rearrange("p (kt c) -> p kt c", kt=DTL),
                                W["WkT"][h].rearrange("(kt p) c -> p kt c", p=128)), 16)
            lkv.inc(KVE.dma_start(wk[:, base + 2048:base + 4096].rearrange("p (kt c) -> p kt c", kt=DTL),
                                W["WvT"][h].rearrange("(kt p) c -> p kt c", p=128)), 16)
            kv_ready[h] = lkv.n

        load_kv(0)
        load_kv(1)
        tpm = [0] * 8
        vpm = [0, 0, 0, 0]
        for h in range(HEADS):
            base = WSB[h % 2]
            t.wait_ge(lkv.h, kv_ready[h])
            v.wait_ge(kvw.h, kvw.n)
            for pj, wsoff, outoff in (("k", 0, QH), ("v", 2048, ATTO)):
                for tc in range(4):
                    psum = P2A if tc % 2 == 0 else P2B
                    key = "A0" if tc % 2 == 0 else "B0"
                    self.pwait(t, key)
                    mm = proj(psum, wk[:, base + wsoff:base + wsoff + 2048], xbr[pj],
                              bias_rows[pj][:, h * 128:h * 128 + 128],
                              asl[pj][:, h * 128:(h + 1) * 128], tc, slots3[pj])
                    T.inc(mm)
                    self.wT(v)
                    V.inc(v.tensor_copy(wk[:, outoff + tc * 512:outoff + (tc + 1) * 512],
                                        psum[:, 0:512]))
                    self.apb[key] = (V.h, V.n)
                    if pj == "v":
                        vpm[tc] = V.n
            kv_t_after[h % 2] = T.n
            if h + 2 < HEADS:
                load_kv(h + 2)
            self.wV(g)
            self.pair_write(
                lambda r, h=h: self.sh_kv[(r * 2 + 0) * HEADS * 128 + h * 128:
                                          (r * 2 + 0) * HEADS * 128 + (h + 1) * 128, :],
                wk[:, QH:QH + 2048], kvw)
            for jg in range(2):
                self.wV(t)
                for j8 in range(8):
                    j = jg * 8 + j8
                    T.inc(t.transpose(PBT[:, j8 * 128:(j8 + 1) * 128],
                                      wk[:, ATTO + j * 128:ATTO + (j + 1) * 128], self.idn))
                self.wT(v)
                V.inc(v.tensor_copy(wk[:, VTM + jg * 1024:VTM + (jg + 1) * 1024],
                                    PBT[:, 0:1024]))
            self.wV(g)
            self.pair_write(
                lambda r, h=h: self.sh_kv[(r * 2 + 1) * HEADS * 128 + h * 128:
                                          (r * 2 + 1) * HEADS * 128 + (h + 1) * 128, :],
                wk[:, VTM:VTM + 2048], kvw)
        g.wait_ge(kvw.h, kvw.n)
        if STOP_AT < 4:
            return
        self.send_flag()
        self.wait_flag()
        if STOP_AT < 5:
            return

        sc = Sem(nc, "sc")
        ex = Sem(nc, "ex")
        av = Sem(nc, "av")
        lqw = Sem(nc, "lqw")
        lkf = Sem(nc, "lkf")
        lvt = Sem(nc, "lvt")
        self.wT(y)
        self.wT(g)
        zb = ar3[:, 3072:5120].bitcast(F32)
        xbo_f = ar3[:, 7168:11264].bitcast(F32)
        self.wS(v)
        V.inc(v.memset(xbo_f[0:16, 0:2048], 0.0))
        KQ = 32
        WSQ = {0: WS, 1: WS + 2048}
        qp_mark = {0: 0, 1: 0}
        wq_ready = {}
        kf_ready = {}
        vt_ready = {}

        def load_wq(h):
            if qp_mark[h % 2]:
                y.wait_ge(T.h, qp_mark[h % 2])
            lqw.inc(y.dma_start(
                wk[:, WSQ[h % 2]:WSQ[h % 2] + 2048].rearrange("p (kt c) -> p kt c", kt=DTL),
                W["WqT"][h].rearrange("(kt p) c -> p kt c", p=128)), 16)
            wq_ready[h] = lqw.n

        def load_kf(h, w):
            if w:
                g.wait_ge(w[0], w[1])
            for r in range(2):
                b_ = (r * 2 + 0) * HEADS * 128 + h * 128
                lkf.inc(g.dma_start(wk[:, KF + r * 2048:KF + (r + 1) * 2048],
                                    self.sh_kv[b_:b_ + 128, :]), 16)
            kf_ready[h] = lkf.n

        def load_vt(h, w):
            if w:
                g.wait_ge(w[0], w[1])
            for r in range(2):
                b_ = (r * 2 + 1) * HEADS * 128 + h * 128
                lvt.inc(g.dma_start(wk[:, VTM + r * 2048:VTM + (r + 1) * 2048],
                                    self.sh_kv[b_:b_ + 128, :]), 16)
            vt_ready[h] = lvt.n

        load_wq(0)
        load_wq(1)
        load_kf(0, None)
        load_vt(0, None)
        for h in range(HEADS):
            hq = h % 2
            t.wait_ge(lqw.h, wq_ready[h])
            for tc in range(4):
                psum = P2A if tc % 2 == 0 else P2B
                key = "A0" if tc % 2 == 0 else "B0"
                self.pwait(t, key)
                mm = proj(psum, wk[:, WSQ[hq]:WSQ[hq] + 2048], xbr["q"],
                          bias_rows["q"][:, h * 128:h * 128 + 128],
                          asl["q"][:, h * 128:(h + 1) * 128], tc, 0)
                T.inc(mm)
                v.wait_ge(T.h, T.n)
                V.inc(v.tensor_copy(wk[:, QH + tc * 512:QH + (tc + 1) * 512], psum[:, 0:512]))
                self.apb[key] = (V.h, V.n)
            qp_mark[hq] = T.n
            if h + 2 < HEADS:
                load_wq(h + 2)
            t.wait_ge(lkf.h, kf_ready[h])
            for qc in range(2):
                q0 = qc * 1024
                sc0, ex0, av0 = sc.n, ex.n, av.n
                self.wV(s)
                for k in range(KQ):
                    s.wait_ge(sc.h, sc0 + k + 1)
                    if k >= 2:
                        s.wait_ge(av.h, av0 + k - 1)
                    ptile = ar3[:, 5120 + (k % 2) * 1024:5120 + (k % 2 + 1) * 1024]
                    ex.inc(s.activation(ptile, (P2A if k % 2 == 0 else P2B)[:, 0:1024],
                                        AF.Exp, bias=EXP_SHIFT, scale=ATT_SCALE))
                def avz(k, first):
                    if first:
                        self.pwait(t, "C0", "C1", "P1")
                        t.wait_ge(lvt.h, vt_ready[h])
                    ptile = ar3[:, 5120 + (k % 2) * 1024:5120 + (k % 2 + 1) * 1024]
                    vblk = wk[:, VTM + k * 128:VTM + (k + 1) * 128]
                    t.matmul(P2C[:, 0:512], vblk, ptile[:, 0:512],
                             start=first, stop=(k == KQ - 1))
                    t.matmul(P2C[:, 512:1024], vblk, ptile[:, 512:1024],
                             start=first, stop=(k == KQ - 1))
                    t.matmul(P1A[0:33, 0:512], self.on33, ptile[:, 0:512],
                             start=first, stop=False)
                    mm = t.matmul(P1A[0:33, 0:512], self.zz33, ptile[:, 512:1024],
                                  start=False, stop=(k == KQ - 1))
                    av.inc(mm)
                for k in range(KQ):
                    if k >= 2:
                        t.wait_ge(ex.h, ex0 + k - 1)
                    if k == 0:
                        self.pwait(t, "A0", "A1")
                    if k == 1:
                        self.pwait(t, "B0")
                    ksl = wk[:, KF + k * 128:KF + (k + 1) * 128]
                    t.matmul((P2A if k % 2 == 0 else P2B)[:, 0:512], ksl,
                             wk[:, QH + q0:QH + q0 + 512], start=True, stop=True)
                    mm = t.matmul((P2A if k % 2 == 0 else P2B)[:, 512:1024], ksl,
                                  wk[:, QH + q0 + 512:QH + q0 + 1024], start=True, stop=True)
                    sc.inc(mm)
                    if k >= 1:
                        t.wait_ge(ex.h, ex0 + k)
                        avz(k - 1, first=(k == 1))
                if qc == 1 and h + 1 < HEADS:
                    load_kf(h + 1, (sc.h, sc.n))
                t.wait_ge(ex.h, ex0 + KQ)
                avz(KQ - 1, first=False)
                v.wait_ge(av.h, av.n)
                zr0 = self.rows[0:1, 0:256].bitcast(BF16)
                zr1 = self.rows[32:33, 0:256].bitcast(BF16)
                with nc.allow_low_precision(reason="bf16 1/Z row (within bf16 budget)"):
                    V.inc(v.reciprocal(zr0, P1A[0:1, 0:512]))
                    V.inc(v.reciprocal(zr1, P1A[32:33, 0:512]))
                self.apb["P1"] = (V.h, V.n)
                self.wV(t)
                T.inc(t.matmul(P2A[:, 0:512], self.bca[0:1, 2048:2176], zr0,
                               start=True, stop=True))
                T.inc(t.matmul(P2A[:, 512:1024], self.bca[32:33, 2048:2176], zr1,
                               start=True, stop=True))
                v.wait_ge(T.h, T.n)
                V.inc(v.tensor_copy(zb[:, 0:1024], P2A[:, 0:1024]))
                self.apb["A0"] = self.apb["A1"] = (V.h, V.n)
                self.wV(v)
                V.inc(v.tensor_mul(wk[:, ATTO + hq * 2048 + q0:ATTO + hq * 2048 + q0 + 1024],
                                   P2C[:, 0:1024], zb[:, 0:1024]))
                self.apb["C0"] = self.apb["C1"] = (V.h, V.n)
            if h + 1 < HEADS:
                load_vt(h + 1, (av.h, av.n))
            if hq == 1 and STOP_AT >= 6:
                self._wo_quarter(h // 2, bsl, asl)
        if STOP_AT >= 6 and (WO_PARTS & 4):
            self._wo_final(bias_rows, asl)

    def pwait(self, eng, *keys):
        for k in keys:
            m = self.apb.get(k)
            if m:
                eng.wait_ge(m[0], m[1])

    def _wo_quarter(self, qi, bsl, asl):
        nc = self.nc
        t, v, g, s, y = self.t, self.v, self.g, self.s, self.y
        T, V = self.T, self.V
        wk, ar3 = self.ar2, self.ar3
        ATTO, WOS = 0, 18432
        P2A, P2B, P2C = self.P2A, self.P2B, self.P2C
        xbo_f = ar3[:, 7168:11264].bitcast(F32)
        t.wait_ge(V.h, V.n)
        for tc in range(4 if (WO_PARTS & 1) else 0):
            ps = P2A if tc % 2 == 0 else P2B
            key = "A0" if tc % 2 == 0 else "B0"
            self.pwait(t, key)
            for j in range(2):
                kt = qi * 2 + j
                mm = t.matmul(ps[0:16, 0:512], bsl["o"][:, kt * 16:(kt + 1) * 16],
                              wk[:, ATTO + j * 2048 + tc * 512:ATTO + j * 2048 + (tc + 1) * 512],
                              start=(j == 0), stop=(j == 1))
            T.inc(mm)
            v.wait_ge(T.h, T.n)
            V.inc(v.tensor_add(xbo_f[0:16, tc * 512:(tc + 1) * 512],
                               xbo_f[0:16, tc * 512:(tc + 1) * 512], ps[0:16, 0:512]))
            self.apb[key] = (V.h, V.n)
        if not hasattr(self, "wo_ldw"):
            self.wo_ldw = Sem(nc, "ldwo")
            self.wo_ready = {}
            self.wo_t_after = {0: 0, 1: 0}
        ldw = self.wo_ldw

        def load_wo(ot_):
            slot = ot_ % 2
            if self.wo_t_after[slot]:
                y.wait_ge(T.h, self.wo_t_after[slot])
            ldw.inc(y.dma_start(
                wk[:, WOS + slot * 256:WOS + slot * 256 + 256].rearrange("p (j c) -> p j c", j=2),
                self.W["WoT"][ot_][qi * 256:(qi + 1) * 256, :]
                .rearrange("(j p) c -> p j c", p=128)), 16)
            self.wo_ready[ot_] = ldw.n

        load_wo(0)
        for ot in range(DTL if (WO_PARTS & 2) else 0):
            slot = ot % 2
            t.wait_ge(ldw.h, self.wo_ready[ot])
            if ot + 1 < DTL:
                load_wo(ot + 1)
            for tc in range(4):
                key = "C0" if tc % 2 == 0 else "C1"
                coff = (tc % 2) * 512
                self.pwait(t, key)
                for j in range(2):
                    mm = t.matmul(P2C[:, coff:coff + 512],
                                  wk[:, WOS + slot * 256 + j * 128:WOS + slot * 256 + (j + 1) * 128],
                                  wk[:, ATTO + j * 2048 + tc * 512:ATTO + j * 2048 + (tc + 1) * 512],
                                  start=(j == 0), stop=(j == 1))
                T.inc(mm)
                v.wait_ge(T.h, T.n)
                V.inc(v.tensor_add(self.xs_t(ot)[:, tc * 512:(tc + 1) * 512],
                                   P2C[:, coff:coff + 512],
                                   self.xs_t(ot)[:, tc * 512:(tc + 1) * 512]))
                self.apb[key] = (V.h, V.n)
            self.wo_t_after[slot] = T.n

    def _wo_final(self, bias_rows, asl):
        nc = self.nc
        t, v, s, g = self.t, self.v, self.s, self.g
        T, V, S = self.T, self.V, self.S
        ar3 = self.ar3
        P2C = self.P2C
        xbo_f = ar3[:, 7168:11264].bitcast(F32)
        xbo_b = self.rowsb[0:16, 0:2048]
        aslo = ar3[0:16, 1024:3072]
        bo_row = self.bca[0:1, 0:2048]
        lf = Sem(nc, self.lbl("ldwof"))
        self.wT(g)
        lf.inc(g.dma_start(aslo, self.W["AoT"][:]), 16)
        lf.inc(g.dma_start(bo_row, self.W["bor"][:]), 16)
        self.wV(s)
        self.wS(s)
        S.inc(s.activation(xbo_b, xbo_f[0:16, 0:2048], AF.Identity, scale=LORA_SCALE))
        self.wS(t)
        t.wait_ge(lf.h, lf.n)
        for ot in range(DTL):
            for tc in range(4):
                key = "C0" if tc % 2 == 0 else "C1"
                coff = (tc % 2) * 512
                self.pwait(t, key)
                t.matmul(P2C[:, coff:coff + 512], aslo[:, ot * 128:(ot + 1) * 128],
                         xbo_b[:, tc * 512:(tc + 1) * 512], start=True, stop=False)
                mm = t.matmul(P2C[:, coff:coff + 512],
                              bo_row[:, ot * 128:(ot + 1) * 128],
                              self.bca[0:1, 2048:2560], start=False, stop=True)
                T.inc(mm)
                v.wait_ge(T.h, T.n)
                V.inc(v.tensor_add(self.xs_t(ot)[:, tc * 512:(tc + 1) * 512],
                                   P2C[:, coff:coff + 512],
                                   self.xs_t(ot)[:, tc * 512:(tc + 1) * 512]))
                self.apb[key] = (V.h, V.n)

    def moe(self, store_cb=None):
        nc = self.nc
        t, v, s, g, y = self.t, self.v, self.s, self.g, self.y
        T, V, S, G, Y = self.T, self.V, self.S, self.G, self.Y
        P2A, P2B, P2C, P1A, PBT = self.P2A, self.P2B, self.P2C, self.P1A, self.PBT
        wk, ar3 = self.ar2, self.ar3
        W = self.W
        ACT2, CWB, DSL = 0, 16384, 18432
        GSL, USL, RW3, SCR3 = 0, 4096, 8192, 8320

        lr = Sem(nc, self.lbl("ldrw"))
        self.wT(g)
        self.wS(g)
        self.wV(g)
        lr.inc(g.dma_start(ar3[:, RW3:RW3 + 128].rearrange("p (kt e) -> p kt e", kt=DTL),
                           W["rw"][:].rearrange("(kt p) e -> p kt e", p=128)), 16)
        t.wait_ge(lr.h, lr.n)
        cwT = self.bca[0:8, 0:2048]
        iota8 = self.misc[:, 132:148].bitcast(F32)
        sct = ar3[:, SCR3:SCR3 + 16].bitcast(F32)
        w1t = ar3[:, SCR3 + 16:SCR3 + 18].bitcast(F32)
        w2t = ar3[:, SCR3 + 18:SCR3 + 20].bitcast(F32)
        m1t = ar3[:, SCR3 + 20:SCR3 + 22].bitcast(F32)
        m2t = ar3[:, SCR3 + 22:SCR3 + 24].bitcast(F32)
        eqt = ar3[:, SCR3 + 24:SCR3 + 40].bitcast(F32)
        tt1 = ar3[:, SCR3 + 40:SCR3 + 56].bitcast(F32)
        i1t = ar3[:, SCR3 + 56:SCR3 + 58].bitcast(F32)
        f1t = ar3[:, SCR3 + 58:SCR3 + 74].bitcast(F32)
        s2t = ar3[:, SCR3 + 74:SCR3 + 90].bitcast(F32)
        f2t = ar3[:, SCR3 + 90:SCR3 + 106].bitcast(F32)
        cwf = ar3[:, SCR3 + 106:SCR3 + 122].bitcast(F32)
        et1 = ar3[:, SCR3 + 122:SCR3 + 126].bitcast(F32)
        cwp = ar3[:, SCR3 + 128:SCR3 + 256]
        sil = ar3[:, SCR3 + 256:SCR3 + 768]
        utm = ar3[:, SCR3 + 768:SCR3 + 1280]

        def vb(inst):
            V.inc(inst)
            self.wV(v)

        for blk in range(NBLK):
            bsl_ = slice(blk * 128, (blk + 1) * 128)
            self.wV(t)
            self.wS(t)
            for kt in range(DTL):
                mm = t.matmul(P1A[:, 0:8], self.lna_t(kt)[:, bsl_],
                              ar3[:, RW3 + kt * 8:RW3 + (kt + 1) * 8],
                              start=(kt == 0), stop=(kt == DTL - 1))
            T.inc(mm)
            self.wT(v)
            vb(v.tensor_copy(sct, P1A[:, 0:8]))
            vb(v.reduce_max(m1t[:, 0:1], sct, axis=AXX))
            vb(v.tensor_tensor(eqt, sct, m1t[:, 0:1].broadcast_to([128, 8]), ALU.is_equal))
            vb(v.scalar_tensor_tensor(tt1, eqt, -1000.0, iota8, ALU.mult, ALU.add))
            vb(v.tensor_reduce(i1t[:, 0:1], tt1, AXX, ALU.min))
            vb(v.tensor_tensor(f1t, tt1, i1t[:, 0:1].broadcast_to([128, 8]), ALU.is_equal))
            vb(v.scalar_tensor_tensor(s2t, f1t, -1e30, sct, ALU.mult, ALU.add))
            vb(v.reduce_max(m2t[:, 0:1], s2t, axis=AXX))
            vb(v.tensor_tensor(eqt, s2t, m2t[:, 0:1].broadcast_to([128, 8]), ALU.is_equal))
            vb(v.scalar_tensor_tensor(tt1, eqt, -1000.0, iota8, ALU.mult, ALU.add))
            vb(v.tensor_reduce(i1t[:, 0:1], tt1, AXX, ALU.min))
            vb(v.tensor_tensor(f2t, tt1, i1t[:, 0:1].broadcast_to([128, 8]), ALU.is_equal))
            vb(v.tensor_sub(et1[:, 0:1], m2t[:, 0:1], m1t[:, 0:1]))
            self.wV(s)
            S.inc(s.activation(et1[:, 1:2], et1[:, 0:1], AF.Exp))
            self.wS(v)
            vb(v.tensor_scalar_add(et1[:, 0:1], et1[:, 1:2], 1.0))
            vb(v.reciprocal(w1t[:, 0:1], et1[:, 0:1]))
            vb(v.tensor_mul(w2t[:, 0:1], et1[:, 1:2], w1t[:, 0:1]))
            vb(v.tensor_scalar_mul(cwf, f1t, w1t[:, 0:1]))
            vb(v.scalar_tensor_tensor(cwf, f2t, w2t[:, 0:1], cwf, ALU.mult, ALU.add))
            vb(v.tensor_copy(cwp[:, 0:8], cwf))
            self.wV(t)
            T.inc(t.transpose(PBT[:, 0:128], cwp, self.idn))
            self.wT(v)
            vb(v.tensor_copy(cwT[:, bsl_], PBT[0:8, 0:128]))

        lgu = Sem(nc, self.lbl("lgu"))
        ldn = Sem(nc, self.lbl("ldn"))
        buf_t_after = {0: 0, 1: 0}

        def load_gu(e, ft):
            bufp = ft % 2
            lgu.inc(y.dma_start(
                ar3[:, GSL + bufp * 2048:GSL + (bufp + 1) * 2048]
                .rearrange("p (kt c) -> p kt c", kt=DTL),
                W["gate"][e][:, ft * 128:(ft + 1) * 128]
                .rearrange("(kt p) c -> p kt c", p=128)), 16)
            lgu.inc(y.dma_start(
                ar3[:, USL + bufp * 2048:USL + (bufp + 1) * 2048]
                .rearrange("p (kt c) -> p kt c", kt=DTL),
                W["up"][e][:, ft * 128:(ft + 1) * 128]
                .rearrange("(kt p) c -> p kt c", p=128)), 16)
            return lgu.n

        self.wV(y)
        self.wT(y)
        self.dn_t_after = {0: 0, 1: 0}
        self.dn_ready = {}
        self.gu_vmark = {0: 0, 1: 0}
        self.gu_smark = {0: 0, 1: 0}
        gu_ready = {(0, 0): load_gu(0, 0)}
        for e in range(MOE_NE):
            self.wV(t)
            for tc in range(4):
                T.inc(t.matmul(P1A[0:1, 0:512], self.idn[0:8, e:e + 1],
                               cwT[:, tc * 512:(tc + 1) * 512], start=True, stop=True))
                self.wT(v)
                cwrb = self.rows[0:1, 0:256].bitcast(BF16)
                V.inc(v.tensor_copy(cwrb, P1A[0:1, 0:512]))
                self.wV(t)
                T.inc(t.matmul(P2C[:, (tc % 2) * 512:(tc % 2 + 1) * 512],
                               self.bca[0:1, 2048:2176], cwrb,
                               start=True, stop=True))
                self.wT(v)
                V.inc(v.tensor_copy(wk[:, CWB + tc * 512:CWB + (tc + 1) * 512],
                                   P2C[:, (tc % 2) * 512:(tc % 2 + 1) * 512]))
                self.wV(t)
            for ft in range(8):
                bufp = ft % 2
                t.wait_ge(lgu.h, gu_ready[(e, ft)])
                ne, nf = (e, ft + 1) if ft < 7 else (e + 1, 0)
                if ne < MOE_NE:
                    if buf_t_after[nf % 2]:
                        y.wait_ge(T.h, buf_t_after[nf % 2])
                    gu_ready[(ne, nf)] = load_gu(ne, nf)
                for tc in range(4):
                    ps = P2A if tc % 2 == 0 else P2B
                    sl = slice(tc * 512, (tc + 1) * 512)
                    bk = tc % 2
                    if self.gu_vmark[bk]:
                        t.wait_ge(V.h, self.gu_vmark[bk])
                        t.wait_ge(S.h, self.gu_smark[bk])
                    for kt in range(DTL):
                        t.matmul(ps[:, 0:512],
                                 ar3[:, GSL + bufp * 2048 + kt * 128:GSL + bufp * 2048 + (kt + 1) * 128],
                                 self.lna_t(kt)[:, sl], start=(kt == 0), stop=(kt == DTL - 1))
                    for kt in range(DTL):
                        mm = t.matmul(ps[:, 512:1024],
                                      ar3[:, USL + bufp * 2048 + kt * 128:USL + bufp * 2048 + (kt + 1) * 128],
                                      self.lna_t(kt)[:, sl], start=(kt == 0), stop=(kt == DTL - 1))
                    T.inc(mm)
                    self.wT(s)
                    self.wV(s)
                    S.inc(s.activation(sil, ps[:, 0:512], AF.Silu))
                    self.gu_smark[bk] = S.n
                    self.wT(v)
                    V.inc(v.tensor_mul(utm, ps[:, 512:1024],
                                       wk[:, CWB + tc * 512:CWB + (tc + 1) * 512]))
                    self.wS(v)
                    self.wV(v)
                    V.inc(v.tensor_mul(
                        wk[:, ACT2 + ft * 2048 + tc * 512:ACT2 + ft * 2048 + (tc + 1) * 512],
                        sil, utm))
                    self.gu_vmark[bk] = V.n
                buf_t_after[ft % 2] = T.n
            self.wV(t)

            def load_dn(ot_):
                g.wait_ge(T.h, self.dn_t_after[ot_ % 2])
                ldn.inc(g.dma_start(
                    wk[:, DSL + (ot_ % 2) * 1024:DSL + (ot_ % 2 + 1) * 1024]
                    .rearrange("p (kt c) -> p kt c", kt=8),
                    W["down"][e][ot_].rearrange("(kt p) c -> p kt c", p=128)), 16)
                self.dn_ready[ot_] = ldn.n
            load_dn(0)
            vm = {0: 0, 1: 0}
            for ot in range(DTL):
                bufp = ot % 2
                t.wait_ge(ldn.h, self.dn_ready[ot])
                if ot + 1 < DTL:
                    load_dn(ot + 1)
                for tc in range(4):
                    ps = P2C if tc % 2 == 0 else P1A
                    if vm[tc % 2]:
                        t.wait_ge(V.h, vm[tc % 2])
                    for kt in range(8):
                        mm = t.matmul(ps[:, 0:512],
                                      wk[:, DSL + bufp * 1024 + kt * 128:DSL + bufp * 1024 + (kt + 1) * 128],
                                      wk[:, ACT2 + kt * 2048 + tc * 512:ACT2 + kt * 2048 + (tc + 1) * 512],
                                      start=(kt == 0), stop=(kt == 7))
                    T.inc(mm)
                    self.wT(v)
                    V.inc(v.tensor_add(self.xs_t(ot)[:, tc * 512:(tc + 1) * 512],
                                       ps[:, 0:512],
                                       self.xs_t(ot)[:, tc * 512:(tc + 1) * 512]))
                    vm[tc % 2] = V.n
                if store_cb is not None and e == MOE_NE - 1:
                    store_cb(ot, V.n)
                self.dn_t_after[bufp] = T.n
            self.wV(t)

    def s6(self):
        nc = self.nc
        t, v, s, g, y = self.t, self.v, self.s, self.g, self.y
        T, V, S, G, Y = self.T, self.V, self.S, self.G, self.Y
        P2A, P2B, P2C, P1A, PBT = self.P2A, self.P2B, self.P2C, self.P1A, self.PBT
        wk, ar3 = self.ar2, self.ar3
        W = self.W
        self.own_y = nc.dram_tensor("own_y", [8 * 128, TOK], BF16, kind="Internal").ap()

        lb = Sem(nc, self.lbl("ls6b"))
        lb.inc(g.dma_start(self.acol[:, 260:276], W["dt_bias"][:].rearrange("e p one -> p (e one)"),
                           allow_slow_non_contiguous=True), 16)
        lb.inc(g.dma_start(self.acol[:, 276:292], W["obias"][:].rearrange("e p one -> p (e one)"),
                           allow_slow_non_contiguous=True), 16)
        lb.inc(g.dma_start(self.acol[:, 292:308], W["Dvec"][:].rearrange("e p one -> p (e one)"),
                           allow_slow_non_contiguous=True), 16)
        le, lo, ldn_ = self.lbl("laE"), self.lbl("laO"), self.lbl("laD")
        nc.br_eq(self.role_r, 0, le, lo, engines=POOL)
        asrc = W["A_log"][:]
        with nc.body(le, valid_engines=POOL):
            lb.inc(g.dma_start(
                self.acol[:, 0:128].rearrange("p (db n) -> p db n", db=8),
                asrc[0:1024, :].rearrange("(db p) n -> p db n", p=128)), 16)
            g.br(ldn_)
        with nc.body(lo, valid_engines=POOL):
            g.dma_start(
                self.acol[:, 0:128].rearrange("p (db n) -> p db n", db=8),
                asrc[1024:2048, :].rearrange("(db p) n -> p db n", p=128)).then_inc(lb.h, 16)
            g.br(ldn_)
        nc.switch_bb(ldn_)
        self.wG(s)
        s.wait_ge(lb.h, lb.n)
        S.inc(s.activation(self.acol[:, 0:128], self.acol[:, 0:128], AF.Exp, scale=1.0))
        self.wS(v)
        V.inc(v.tensor_scalar_mul(self.acol[:, 0:128], self.acol[:, 0:128], -1.0))
        self.wV(s)

        self._s6_pre()
        v.wait_ge(lb.h, lb.n)
        self.wV(v)
        for ot in range(DTL):
            V.inc(v.scalar_tensor_tensor(self.xs_t(ot), self.lna_t(ot),
                                         self.acol[:, 292 + ot:293 + ot],
                                         self.xs_t(ot), ALU.mult, ALU.add))
            self.wV(v)
        self._s6_scan()

    def _s6_pre(self):
        nc = self.nc
        t, v, s, g, y = self.t, self.v, self.s, self.g, self.y
        T, V, S, G = self.T, self.V, self.S, self.G
        P2A, P2B, P1A = self.P2A, self.P2B, self.P1A
        wk, ar3 = self.ar2, self.ar3
        W = self.W
        dxw = Sem(nc, "dxw")
        DTD, DTW, S1T, DLT, DXT = 0, 2048, 4096, 6144, 8192
        ld = Sem(nc, self.lbl("ls6p"))
        self.wT(g)
        self.wV(g)
        self.wS(g)
        ld.inc(g.dma_start(wk[:, DTD:DTD + 2048].rearrange("p (kt c) -> p kt c", kt=DTL),
                           W["dt_down"][:].rearrange("(kt p) c -> p kt c", p=128)), 16)
        ld.inc(g.dma_start(wk[:, DTW:DTW + 2048].rearrange("p (et c) -> p et c", et=DTL),
                           W["dtwT"][:].rearrange("et r c -> r et c")), 16)
        ld.inc(g.dma_start(ar3[:, 0:512].rearrange("p (kt c) -> p kt c", kt=DTL),
                           W["WBC"][:].rearrange("(kt p) c -> p kt c", p=128)), 16)
        t.wait_ge(ld.h, ld.n)
        for tc in range(4):
            ps = P2A if tc % 2 == 0 else P2B
            self.wS(t)
            for kt in range(DTL):
                mm = t.matmul(ps[:, 0:512], wk[:, DTD + kt * 128:DTD + (kt + 1) * 128],
                              self.lna_t(kt)[:, tc * 512:(tc + 1) * 512],
                              start=(kt == 0), stop=(kt == DTL - 1))
            for kt in range(DTL):
                mm2 = t.matmul(P1A[0:32, 0:512], ar3[:, kt * 32:(kt + 1) * 32],
                               self.lna_t(kt)[:, tc * 512:(tc + 1) * 512],
                               start=(kt == 0), stop=(kt == DTL - 1))
            T.inc(mm)
            T.inc(mm2)
            self.wT(s)
            S.inc(s.activation(wk[:, S1T + tc * 512:S1T + (tc + 1) * 512], ps[:, 0:512],
                               AF.Identity))
            self.wT(v)
            V.inc(v.tensor_copy(ar3[0:32, 512 + tc * 512:512 + (tc + 1) * 512],
                                P1A[0:32, 0:512]))
            self.wV(t)
            self.wS(t)
        self.wV(g)
        self.pair_write(lambda r: self.sh_bc[r * 32:r * 32 + 32, :],
                        ar3[0:32, 512:2560], dxw)
        for et in range(DTL):
            for tc in range(4):
                ps = P2A if tc % 2 == 0 else P2B
                self.wS(t)
                self.wV(t)
                mm = t.matmul(ps[:, 0:512], wk[:, DTW + et * 128:DTW + (et + 1) * 128],
                              wk[:, S1T + tc * 512:S1T + (tc + 1) * 512],
                              start=True, stop=True)
                T.inc(mm)
                self.wT(s)
                dtb = self.acol[:, 260 + et:261 + et]
                ab = ar3[:, 2560:3072]
                rp = ar3[:, 3072:3584]
                S.inc(s.activation(ab, ps[:, 0:512], AF.Abs, bias=dtb))
                S.inc(s.activation(rp, ps[:, 0:512], AF.Relu, bias=dtb))
                self.wS(s)
                S.inc(s.activation(ab, ab, AF.Exp, scale=-1.0))
                self.wS(s)
                S.inc(s.activation(ab, ab, AF.Ln, bias=1.0))
                self.wS(v)
                dsl_ = slice(DLT + tc * 512, DLT + (tc + 1) * 512)
                xsl_ = slice(DXT + tc * 512, DXT + (tc + 1) * 512)
                V.inc(v.tensor_add(wk[:, dsl_], rp, ab))
                self.wV(v)
                V.inc(v.tensor_mul(wk[:, xsl_], wk[:, dsl_],
                                   self.lna_t(et)[:, tc * 512:(tc + 1) * 512]))
                self.wV(t)
                self.wV(s)
            self.wV(g)
            self.pair_write(lambda r, et=et: self.sh_dx[(r * 2 + 0) * D + et * 128:
                                                        (r * 2 + 0) * D + (et + 1) * 128, :],
                            wk[:, DLT:DLT + 2048], dxw)
            self.pair_write(lambda r, et=et: self.sh_dx[(r * 2 + 1) * D + et * 128:
                                                        (r * 2 + 1) * D + (et + 1) * 128, :],
                            wk[:, DXT:DXT + 2048], dxw)
            g.wait_ge(dxw.h, dxw.n)
        self.send_flag()
        self.wait_flag()

    def _s6_scan(self):
        nc = self.nc
        t, v, s, g, y = self.t, self.v, self.s, self.g, self.y
        T, V, S, G = self.T, self.V, self.S, self.G
        P2C, P1A = self.P2C, self.P1A
        wk, ar3 = self.ar2, self.ar3
        yacc = self.ar1[:, 16384:24576]
        state = self.misc[:, 256:512].bitcast(F32)
        self.wS(v)
        self.wT(v)
        V.inc(v.memset(state, 0.0))
        V.inc(v.memset(yacc, 0.0))
        self.wV(v)
        stg = Sem(nc, self.lbl("stg"))
        svb = Sem(nc, "svb")
        sae = Sem(nc, "sae")
        self.po_lw = Sem(nc, "polw")
        self.po_lym = Sem(nc, "polym")
        self.po_lw_ready = {}
        self.po_lw_t_after = {0: self.T.n, 1: self.T.n}
        self.po_t_mark = {}
        self.po_ps_smark = {0: 0, 1: 0}
        self.po_ro_vmark = 0
        self.po_ymv_ready = 0
        self.po_ymv_t_after = self.T.n
        self._a_free = {0: 0, 1: 0}
        self._bc_smark = {0: 0, 1: 0}
        bufs = ((ar3[:, 8192:9216], ar3[:, 9216:10240]),
                (ar3[:, 5120:6144], ar3[:, 6144:7168]))
        bc_free = {0: 0, 1: 0}
        for q in range(4):
            self.wV(g)
            g.wait_ge(svb.h, svb.n)
            self.wS(g)
            for xi, dst0 in ((0, 0), (1, 8192)):
                for r in (0, 1):
                    le, lo, ldn_ = self.lbl("sgE"), self.lbl("sgO"), self.lbl("sgD")
                    nc.br_eq(self.role_r, 0, le, lo, engines=POOL)
                    for lbl, myh in ((le, 0), (lo, 1)):
                        with nc.body(lbl, valid_engines=POOL):
                            for db in range(8):
                                src = self.sh_dx[(r * 2 + xi) * D + myh * 1024 + db * 128:
                                                 (r * 2 + xi) * D + myh * 1024 + (db + 1) * 128,
                                                 4 * q * 128:(4 * q + 4) * 128]
                                src3 = src.rearrange("p (j c) -> p j c", c=128)
                                dst = self.ar1[:, dst0 + db * 1024 + r * 128:
                                               dst0 + db * 1024 + r * 128 + 896]
                                dst3 = bass.AP(dst.tensor, dst.offset,
                                               [[32768, 128], [256, 4], [1, 128]])
                                inst = g.dma_start(dst3, src3)
                                if myh == 0:
                                    stg.inc(inst, 16)
                                else:
                                    inst.then_inc(stg.h, 16)
                            g.br(ldn_)
                    nc.switch_bb(ldn_)
            for r in (0, 1):
                src = self.sh_bc[r * 32:r * 32 + 32, 4 * q * 128:(4 * q + 4) * 128]
                src3 = src.rearrange("w (j c) -> w j c", c=128)
                dst = ar3[0:32, r * 128:r * 128 + 896]
                dst3 = bass.AP(dst.tensor, dst.offset, [[11264, 32], [256, 4], [1, 128]])
                stg.inc(g.dma_start(dst3, src3), 16)
            if q >= 1:
                self._post_ymov(q - 1)
            v.wait_ge(stg.h, stg.n)
            t.wait_ge(stg.h, stg.n)
            s.wait_ge(stg.h, stg.n)
            self._bc_bcast(0, bufs[0], bc_free[0], svb, 0)
            for n in range(NST):
                if n + 1 < NST:
                    self._bc_bcast(n + 1, bufs[(n + 1) % 2], bc_free[(n + 1) % 2], svb,
                                   (n + 1) % 2)
                if q >= 1:
                    if n >= 6 and n - 6 <= 9:
                        self._post_SR(q - 1, n - 6)
                    if n >= 4 and n - 4 <= 11:
                        self._post_T(n - 4)
                Bb, Cb = bufs[n % 2]
                v.wait_ge(self.S.h, self._bc_smark[n % 2])
                for db in range(8):
                    col = db * NST + n
                    a_t = wk[:, 0:1024] if db % 2 == 0 else wk[:, 16384:17408]
                    u_t = wk[:, 1024:2048]
                    h_t = wk[:, 2048:3072]
                    z_t = wk[:, 3072:4096]
                    s.wait_ge(svb.h, self._a_free[db % 2])
                    sae.inc(s.activation(a_t, self.ar1[:, db * 1024:(db + 1) * 1024],
                                         AF.Exp, scale=self.acol[:, col:col + 1]))
                    svb.inc(v.tensor_mul(u_t, self.ar1[:, 8192 + db * 1024:8192 + (db + 1) * 1024], Bb))
                    v.wait_ge(sae.h, sae.n)
                    v.wait_ge(svb.h, svb.n)
                    svb.inc(v.tensor_tensor_scan(h_t, a_t, u_t, state[:, col:col + 1],
                                                 ALU.mult, ALU.add))
                    self._a_free[db % 2] = svb.n
                    v.wait_ge(svb.h, svb.n)
                    svb.inc(v.tensor_copy(state[:, col:col + 1], h_t[:, 1023:1024]))
                    svb.inc(v.tensor_mul(z_t, h_t, Cb))
                    v.wait_ge(svb.h, svb.n)
                    svb.inc(v.tensor_add(yacc[:, db * 1024:(db + 1) * 1024],
                                         yacc[:, db * 1024:(db + 1) * 1024], z_t))
                    v.wait_ge(svb.h, svb.n)
                bc_free[n % 2] = svb.n
            if q >= 1:
                for i in range(10, 16):
                    self._post_SR(q - 1, i)
                    if i + 2 < 16:
                        self._post_T(i + 2)
            self.wV(g)
            g.wait_ge(svb.h, svb.n)
            self.wV(t)
            yq = Sem(nc, self.lbl("yq"))
            for db in range(8):
                ybase = yacc[:, db * 1024:(db + 1) * 1024]
                le, lo, ldn_ = self.lbl("yoE"), self.lbl("yoO"), self.lbl("yoD")
                nc.br_eq(self.role_r, 0, le, lo, engines=POOL)
                for lbl, myh in ((le, 0), (lo, 1)):
                    with nc.body(lbl, valid_engines=POOL):
                        i1 = g.dma_start(
                            self.own_y[db * 128:(db + 1) * 128, 4 * q * 128:(4 * q + 4) * 128]
                            .rearrange("p (j c) -> p j c", c=128),
                            bass.AP(ybase.tensor, ybase[0:1, myh * 128:myh * 128 + 1].offset,
                                    [[32768, 128], [256, 4], [1, 128]]))
                        i2 = g.dma_start(
                            self.sh_y[(myh * 8 + db) * 128:(myh * 8 + db + 1) * 128,
                                      4 * q * 128:(4 * q + 4) * 128]
                            .rearrange("p (j c) -> p j c", c=128),
                            bass.AP(ybase.tensor,
                                    ybase[0:1, (1 - myh) * 128:(1 - myh) * 128 + 1].offset,
                                    [[32768, 128], [256, 4], [1, 128]]))
                        if myh == 0:
                            yq.inc(i1, 16)
                            yq.inc(i2, 16)
                        else:
                            i1.then_inc(yq.h, 16)
                            i2.then_inc(yq.h, 16)
                        g.br(ldn_)
                nc.switch_bb(ldn_)
            g.wait_ge(yq.h, yq.n)
            self.send_flag()
            if q < 3:
                v.wait_ge(yq.h, yq.n)
                V.inc(v.memset(yacc, 0.0))
                self.wV(v)
        self._post_ymov(3)
        self._post_T(0)
        for ot in range(DTL):
            if ot + 1 < DTL:
                self._post_T(ot + 1)
            self._post_SR(3, ot)

    def _bc_bcast(self, n, buf, free_mark, svb, bufidx):
        t, s = self.t, self.s
        T, S = self.T, self.S
        ar3, P1A, P2C = self.ar3, self.P1A, self.P2C
        t.wait_ge(svb.h, free_mark)
        bcrb = self.rows[0:1, 0:256].bitcast(BF16)
        for ri, dst in ((n, buf[0]), (16 + n, buf[1])):
            for c2 in range(2):
                t.wait_ge(S.h, S.n)
                T.inc(t.matmul(P1A[0:1, 0:512], self.idn[0:32, ri:ri + 1],
                               ar3[0:32, c2 * 512:(c2 + 1) * 512],
                               start=True, stop=True))
                s.wait_ge(T.h, T.n)
                S.inc(s.activation(bcrb, P1A[0:1, 0:512], AF.Identity))
                t.wait_ge(S.h, S.n)
                T.inc(t.matmul(P2C[:, c2 * 512:(c2 + 1) * 512], self.bca[0:1, 2048:2176],
                               bcrb, start=True, stop=True))
                s.wait_ge(T.h, T.n)
                S.inc(s.activation(dst[:, c2 * 512:(c2 + 1) * 512],
                                   P2C[:, c2 * 512:(c2 + 1) * 512], AF.Identity))
        self._bc_smark[bufidx] = S.n

    def _post_ymov(self, qq):
        nc, g = self.nc, self.g
        wk = self.ar2
        self.wait_flag()
        if self.po_ymv_t_after:
            g.wait_ge(self.T.h, self.po_ymv_t_after)
        lym = self.po_lym
        le, lo, ldn_ = self.lbl("ymE"), self.lbl("ymO"), self.lbl("ymD")
        nc.br_eq(self.role_r, 0, le, lo, engines=POOL)
        for lbl, myh in ((le, 0), (lo, 1)):
            with nc.body(lbl, valid_engines=POOL):
                i1 = g.dma_start(
                    wk[:, 4096 + myh * 4096:4096 + myh * 4096 + 4096]
                    .rearrange("p (db c) -> p db c", db=8),
                    self.own_y[:, qq * 512:(qq + 1) * 512]
                    .rearrange("(db p) c -> p db c", p=128))
                i2 = g.dma_start(
                    wk[:, 4096 + (1 - myh) * 4096:4096 + (1 - myh) * 4096 + 4096]
                    .rearrange("p (db c) -> p db c", db=8),
                    self.sh_y[(1 - myh) * 8 * 128:((1 - myh) * 8 + 8) * 128,
                              qq * 512:(qq + 1) * 512]
                    .rearrange("(db p) c -> p db c", p=128))
                if myh == 0:
                    lym.inc(i1, 16)
                    lym.inc(i2, 16)
                else:
                    i1.then_inc(lym.h, 16)
                    i2.then_inc(lym.h, 16)
                g.br(ldn_)
        nc.switch_bb(ldn_)
        self.po_ymv_ready = lym.n
        self._load_wsl(0)

    def _load_wsl(self, ot):
        y = self.y
        wk = self.ar2
        buf = ot % 2
        if self.po_lw_t_after[buf]:
            y.wait_ge(self.T.h, self.po_lw_t_after[buf])
        self.po_lw.inc(y.dma_start(
            wk[:, 12288 + buf * 2048:12288 + (buf + 1) * 2048]
            .rearrange("p (kt c) -> p kt c", kt=DTL),
            self.W["opwT"][ot].rearrange("(kt p) c -> p kt c", p=128)), 16)
        self.po_lw_ready[ot] = self.po_lw.n

    def _post_T(self, ot):
        t = self.t
        T, S = self.T, self.S
        wk = self.ar2
        ps = self.P2A if ot % 2 == 0 else self.P2B
        t.wait_ge(self.po_lym.h, self.po_ymv_ready)
        t.wait_ge(self.po_lw.h, self.po_lw_ready[ot])
        if ot + 1 < DTL:
            self._load_wsl(ot + 1)
        buf = ot % 2
        if self.po_ps_smark[ot % 2]:
            t.wait_ge(S.h, self.po_ps_smark[ot % 2])
        for kt in range(DTL):
            mm = t.matmul(ps[:, 0:512],
                          wk[:, 12288 + buf * 2048 + kt * 128:12288 + buf * 2048 + (kt + 1) * 128],
                          wk[:, 4096 + kt * 512:4096 + (kt + 1) * 512],
                          start=(kt == 0), stop=(kt == DTL - 1))
        T.inc(mm)
        self.po_lw_t_after[buf] = T.n
        self.po_t_mark[ot] = T.n
        if ot == DTL - 1:
            self.po_ymv_t_after = T.n

    def _post_SR(self, qq, ot):
        v, s = self.v, self.s
        V, S = self.V, self.S
        ps = self.P2A if ot % 2 == 0 else self.P2B
        ro = self.ar3[:, 2560:3072]
        s.wait_ge(self.T.h, self.po_t_mark[ot])
        if self.po_ro_vmark:
            s.wait_ge(V.h, self.po_ro_vmark)
        S.inc(s.activation(ro, ps[:, 0:512], AF.Relu,
                           bias=self.acol[:, 276 + ot:277 + ot]))
        self.po_ps_smark[ot % 2] = S.n
        v.wait_ge(S.h, S.n)
        xsl = self.xs_t(ot)[:, qq * 512:(qq + 1) * 512]
        V.inc(v.tensor_add(xsl, ro, xsl))
        self.po_ro_vmark = V.n


def build_nc(detect_races=True):
    nc = bass.Bass("TRN2", target_bir_lowering=False, debug=False,
                   detect_race_conditions=detect_races)
    Prog(nc)
    return nc


def kernel(**inputs):
    nonce = int(np.random.randint(1 << 20, 1 << 29))
    nc = build_nc()
    in_maps = [make_inputs(c, inputs, nonce) for c in range(8)]
    res = run_bass_kernel_spmd(nc, in_maps, core_ids=list(range(8)))
    return reassemble([r for r in res.results])
